# version 01
# baseline (speedup 1.0000x reference)
import numpy as np
import ml_dtypes
from contextlib import ExitStack

import concourse.bass as bass
import concourse.mybir as mybir
from concourse.bass_utils import run_bass_kernel_spmd

F32 = mybir.dt.float32
BF16 = mybir.dt.bfloat16
AF = mybir.ActivationFunctionType
ALU = mybir.AluOpType
AX = mybir.AxisListType
bf = ml_dtypes.bfloat16

T, D = 4096, 1024
NT = 32
ND = 8
NE, SS, FF = 16, 64, 2048
EPS = 1e-5
RING = 3
CHUNK = 512
NCH = T // CHUNK

ENGS = ("sync", "tensor", "vector", "scalar", "gpsimd")


class Plan:
    def __init__(self, nc, stack):
        self.nc = nc
        self.streams = {e: [] for e in ENGS}
        self.sems = {}
        for name in ("qs", "qc", "qg", "pe", "dve", "act", "bar"):
            h = stack.enter_context(nc.semaphore(name))
            self.sems[name] = [h, 0]

    def emit(self, eng, fn):
        self.streams[eng].append(fn)

    def op(self, eng, meth, *args, inc=None, incn=1, **kw):
        if inc is not None:
            s = self.sems[inc]
            s[1] += incn
            h = s[0]
        else:
            h = None

        def f(e, m=meth, a=args, k=kw, h=h, n=incn):
            r = getattr(e, m)(*a, **k)
            if h is not None:
                r.then_inc(h, n)
        self.emit(eng, f)

    def snap(self, sem):
        return (sem, self.sems[sem][1])

    def wait(self, eng, snapshot):
        sem, v = snapshot
        if v <= 0:
            return
        h = self.sems[sem][0]
        self.emit(eng, lambda e, h=h, v=v: e.wait_ge(h, v))

    ENG_SEM = {"tensor": "pe", "vector": "dve", "scalar": "act"}

    def opf(self, eng, meth, *args, **kw):
        sem = self.ENG_SEM[eng]
        self.op(eng, meth, *args, inc=sem, **kw)
        self.wait(eng, self.snap(sem))

    def dma(self, eng, out, in_, q):
        self.op(eng, "dma_start", inc=q, incn=16, out=out, in_=in_)

    def barrier(self):
        s = self.sems["bar"]
        for eng in ENGS:
            self.emit(eng, lambda e, h=s[0]: e.sem_inc(h, 1))
        s[1] += len(ENGS)
        v = s[1]
        for eng in ENGS:
            self.emit(eng, lambda e, h=s[0], v=v: e.wait_ge(h, v))


def _r3(ap, c):
    return ap[:].rearrange("p (a b) -> p a b", b=c)


def build(dbg_stage=3, dump=False, paranoid=False, skip=()):
    nc = bass.Bass("TRN2", target_bir_lowering=False)
    dt_in = {}

    def din(name, shape, dtype=F32):
        t = nc.dram_tensor(name, shape, dtype, kind="ExternalInput")
        dt_in[name] = t.ap()
        return dt_in[name]

    x_in = din("x", [T, D])
    wkT = din("wkT", [D, D]); wvT = din("wvT", [D, D])
    wrT = din("wrT", [D, D]); woT = din("woT", [D, D])
    wg = din("wg", [NE * D, FF]); wu = din("wu", [NE * D, FF])
    wd = din("wd", [NE * FF, D])
    semb = din("semb", [1024, D])
    bgE = din("bgE", [2, 8 * FF]); buE = din("buE", [2, 8 * FF])
    bdE = din("bdE", [2, 8 * D])
    tdP = din("tdP", [128, 8]); tmkP = din("tmkP", [128, 8])
    tmvP = din("tmvP", [128, 8]); tmrP = din("tmrP", [128, 8])
    lng = din("lng", [1, D]); lnb = din("lnb", [1, D])
    nog = din("nog", [1, D]); nob = din("nob", [1, D])
    slg = din("slg", [1, D]); slb = din("slb", [1, D])
    scl = din("scl", [1, 1])
    idb = din("idb", [128, 128], BF16)
    e2d = din("e2", [2, 128], BF16)
    one1 = din("one1", [1, 128], BF16)

    xres = nc.dram_tensor("xres", [T, D], F32).ap()
    y = nc.dram_tensor("y", [T, D], F32, kind="ExternalOutput").ap()
    dbg = {}
    if dump:
        for nm, sh, dt in (("d_xn", [128, NT * 1024], BF16),
                           ("d_xnT", [128, ND * 4096], BF16),
                           ("d_dT", [128, 4096], BF16),
                           ("d_stt", [128, 16], F32),
                           ("d_ssc", [128, 8], F32),
                           ("d_slot", [128, 1024], BF16),
                           ("d_hT", [128, ND * 1024], BF16),
                           ("d_eout", [128, ND * 1024], BF16),
                           ("d_gate", [128, 2048], BF16),
                           ("d_up", [128, 2048], BF16),
                           ("d_gu", [128, 2048], BF16),
                           ("d_guT", [128, 2048], BF16),
                           ("d_seT", [128, ND * 1024], BF16)):
            dbg[nm] = nc.dram_tensor(nm, sh, dt, kind="ExternalOutput").ap()

    with ExitStack() as S:
        E = S.enter_context
        pl = Plan(nc, S)

        idb_s = E(nc.sbuf_tensor("idb_s", [128, 128], BF16))
        e2_s = E(nc.sbuf_tensor("e2_s", [2, 128], BF16))
        one1_s = E(nc.sbuf_tensor("one1_s", [1, 128], BF16))
        seT = E(nc.sbuf_tensor("seT", [128, ND * 1024], BF16))
        tdE = E(nc.sbuf_tensor("tdE", [128, 8], F32))
        tmk_s = E(nc.sbuf_tensor("tmk_s", [128, 8], F32))
        tmv_s = E(nc.sbuf_tensor("tmv_s", [128, 8], F32))
        tmr_s = E(nc.sbuf_tensor("tmr_s", [128, 8], F32))
        vrow = E(nc.sbuf_tensor("vrow", [1, D], F32))
        vrow_b = E(nc.sbuf_tensor("vrow_b", [1, D], BF16))
        sclb = E(nc.sbuf_tensor("sclb", [128, 1], F32))

        pl.dma("sync", idb_s[:], idb, "qs")
        pl.dma("sync", e2_s[:], e2d, "qs")
        pl.dma("sync", one1_s[:], one1, "qs")
        pl.dma("sync", tdE[:], tdP, "qs")
        pl.dma("sync", tmk_s[:], tmkP, "qs")
        pl.dma("sync", tmv_s[:], tmvP, "qs")
        pl.dma("sync", tmr_s[:], tmrP, "qs")
        ld0 = pl.snap("qs")
        pl.wait("scalar", ld0)
        pl.op("scalar", "activation", tdE[:], tdE[:], AF.Exp, inc="act")

        vrow_rel = [None]

        def bcast_vec(vec_dram, dst_bf, ps_pool, tag):
            if vrow_rel[0] is not None:
                pl.wait("sync", vrow_rel[0])
            pl.dma("sync", vrow[:], vec_dram, "qs")
            pl.wait("vector", pl.snap("qs"))
            pl.wait("vector", pl.snap("pe"))
            pl.op("vector", "tensor_copy", vrow_b[:], vrow[:], inc="dve")
            vrow_rel[0] = pl.snap("dve")
            pl.wait("tensor", pl.snap("dve"))
            pl.wait("tensor", ld0)
            for nb in range(2):
                pl.op("tensor", "matmul", ps_pool[:, nb * 512:(nb + 1) * 512],
                      one1_s[:], vrow_b[:, nb * 512:(nb + 1) * 512],
                      start=True, stop=True, inc="pe" if nb == 1 else None)
            pl.wait("vector", pl.snap("pe"))
            pl.op("vector", "tensor_copy", dst_bf[:], ps_pool[:], inc="dve")

        def bcast_scl(ps_pool):
            if vrow_rel[0] is not None:
                pl.wait("sync", vrow_rel[0])
            pl.dma("sync", vrow[:, 0:1], scl, "qs")
            pl.wait("vector", pl.snap("qs"))
            pl.wait("vector", pl.snap("pe"))
            pl.op("vector", "tensor_copy", vrow_b[:, 0:1], vrow[:, 0:1], inc="dve")
            vrow_rel[0] = pl.snap("dve")
            pl.wait("tensor", pl.snap("dve"))
            pl.op("tensor", "matmul", ps_pool[:, 0:1], one1_s[:], vrow_b[:, 0:1],
                  start=True, stop=True, inc="pe")
            pl.wait("vector", pl.snap("pe"))
            pl.op("vector", "tensor_copy", sclb[:], ps_pool[:, 0:1], inc="dve")

        with nc.sbuf_tensor("se_x", [128, 1024], F32) as se_x, \
             nc.sbuf_tensor("se_sq", [128, 1024], F32) as se_sq, \
             nc.sbuf_tensor("se_bf", [128, 1024], BF16) as se_bf, \
             nc.sbuf_tensor("se_t1", [128, 1024], BF16) as se_t1, \
             nc.sbuf_tensor("slgb", [128, 1024], BF16) as slgb, \
             nc.sbuf_tensor("slbb", [128, 1024], BF16) as slbb, \
             nc.sbuf_tensor("st8", [128, 8], F32) as st8, \
             nc.psum_tensor("pre_ps", [128, 1024], F32) as pre_ps, \
             nc.psum_tensor("pre_tp", [128, 1024], BF16) as pre_tp:
            bcast_vec(slg, slgb, pre_ps, "slg")
            bcast_vec(slb, slbb, pre_ps, "slb")
            bcast_scl(pre_ps)
            for st in range(8):
                pl.dma("sync", se_x[:], semb[st * 128:(st + 1) * 128, :], "qs")
                lsn = pl.snap("qs")
                pl.wait("vector", lsn)
                pl.wait("scalar", lsn)
                pl.opf("vector", "reduce_sum", st8[:, 0:1], se_x[:], axis=AX.X)
                s1_done = pl.snap("dve")
                pl.op("scalar", "activation", se_sq[:], se_x[:], AF.Square,
                      accum_out=st8[:, 1:2])
                pl.wait("scalar", s1_done)
                pl.op("scalar", "activation", st8[:, 4:5], st8[:, 0:1],
                      AF.Square, scale=1.0 / D)
                pl.op("scalar", "activation", st8[:, 5:6], st8[:, 1:2],
                      AF.Copy, scale=1.0 / D, bias=EPS, inc="act")
                pl.wait("vector", pl.snap("act"))
                pl.opf("vector", "tensor_tensor", st8[:, 5:6], st8[:, 5:6],
                       st8[:, 4:5], ALU.subtract)
                pl.wait("scalar", pl.snap("dve"))
                pl.op("scalar", "activation", st8[:, 6:7], st8[:, 5:6], AF.Ln)
                pl.op("scalar", "activation", st8[:, 6:7], st8[:, 6:7], AF.Exp,
                      scale=-0.5, inc="act")
                pl.wait("vector", pl.snap("act"))
                pl.opf("vector", "scalar_tensor_tensor", st8[:, 7:8], st8[:, 0:1],
                       1.0 / D, st8[:, 6:7], ALU.mult, ALU.mult)
                pl.opf("vector", "tensor_scalar", se_bf[:], se_x[:],
                       st8[:, 6:7], st8[:, 7:8], ALU.mult, ALU.subtract)
                pl.opf("vector", "tensor_tensor", se_t1[:], se_bf[:], slgb[:], ALU.mult)
                pl.opf("vector", "tensor_tensor", se_t1[:], se_t1[:], slbb[:], ALU.add)
                pl.op("vector", "tensor_scalar", se_t1[:], se_t1[:],
                      sclb[:, 0:1], None, ALU.mult, inc="dve")
                pl.wait("tensor", pl.snap("dve"))
                for d in range(8):
                    pl.op("tensor", "transpose", pre_tp[:, d * 128:(d + 1) * 128],
                          se_t1[:, d * 128:(d + 1) * 128], idb_s[:],
                          inc="pe" if d == 7 else None)
                pl.wait("vector", pl.snap("pe"))
                pl.op("vector", "tensor_copy",
                      _r3(seT, 1024)[:, :, st * 128:(st + 1) * 128],
                      _r3(pre_tp, 128), inc="dve")
                pl.barrier()
            pl.barrier()
        if dump:
            pl.dma("sync", dbg["d_seT"], seT[:], "qs")

        def ffn(src, dst, u):
            with ExitStack() as FS:
                FE = FS.enter_context
                xnT = FE(nc.sbuf_tensor("xnT" + u, [128, ND * 4096], BF16))
                hT = FE(nc.sbuf_tensor("hT" + u, [128, ND * 1024], BF16))
                xrot = FE(nc.sbuf_tensor("xrot" + u, [128, 2 * 1024], F32))
                stt = FE(nc.sbuf_tensor("stt" + u, [128, 16], F32))
                ssc = FE(nc.sbuf_tensor("ssc" + u, [128, 8], F32))
                rcpS = FE(nc.sbuf_tensor("rcpS" + u, [128, 8], F32))
                ring = FE(nc.sbuf_tensor("ring" + u, [128, RING * 2048], BF16))
                AS2 = FE(ExitStack())
                AE2 = AS2.enter_context
                xn = AE2(nc.sbuf_tensor("xn" + u, [128, NT * 1024], BF16))
                dT = AE2(nc.sbuf_tensor("dT" + u, [128, 4096], BF16))
                stg = AE2(nc.sbuf_tensor("stg" + u, [128, 2 * 512], BF16))
                nogb = AE2(nc.sbuf_tensor("nogb" + u, [128, 1024], BF16))
                nobb = AE2(nc.sbuf_tensor("nobb" + u, [128, 1024], BF16))
                xcb = AE2(nc.sbuf_tensor("xcb" + u, [128, 1024], BF16))
                t1b = AE2(nc.sbuf_tensor("t1b" + u, [128, 1024], BF16))
                slot_bf = AE2(nc.sbuf_tensor("slot_bf" + u, [128, 1024], BF16))

                with nc.psum_tensor("bc_ps" + u, [128, 1024], F32) as bc_ps:
                    bcast_vec(nog, nogb, bc_ps, "nog")
                    bcast_vec(nob, nobb, bc_ps, "nob")

                xrot_rel = [None, None]

                with nc.psum_tensor("ln_tp" + u, [128, 2 * 1024], BF16) as ln_tp, \
                     nc.psum_tensor("sqp" + u, [128, 1024], F32) as sq:
                    for i in range(NT):
                        h = i % 2
                        if xrot_rel[h] is not None:
                            pl.wait("scalar", xrot_rel[h])
                        pl.dma("scalar", xrot[:, h * 1024:(h + 1) * 1024],
                               src[i * 128:(i + 1) * 128, :], "qc")
                        ls = pl.snap("qc")
                        xi = xrot[:, h * 1024:(h + 1) * 1024]
                        pl.wait("vector", ls)
                        pl.wait("scalar", ls)
                        pl.opf("vector", "reduce_sum", stt[:, 0:1], xi, axis=AX.X)
                        s1d = pl.snap("dve")
                        pl.op("scalar", "activation", sq[:, 0:1024], xi, AF.Square,
                              accum_out=stt[:, 1:2])
                        pl.wait("scalar", s1d)
                        pl.op("scalar", "activation", stt[:, 4:5], stt[:, 0:1],
                              AF.Square, scale=(1.0 + EPS) ** 0.5 / D)
                        pl.op("scalar", "activation", stt[:, 5:6], stt[:, 1:2],
                              AF.Copy, scale=(1.0 + EPS) / D, bias=EPS * EPS,
                              inc="act")
                        pl.wait("vector", pl.snap("act"))
                        pl.opf("vector", "tensor_tensor", stt[:, 5:6], stt[:, 5:6],
                               stt[:, 4:5], ALU.subtract)
                        pl.wait("scalar", pl.snap("dve"))
                        pl.op("scalar", "activation", stt[:, 6:7], stt[:, 5:6], AF.Ln)
                        pl.op("scalar", "activation", stt[:, 6:7], stt[:, 6:7],
                              AF.Exp, scale=-0.5, inc="act")
                        pl.wait("vector", pl.snap("act"))
                        pl.opf("vector", "scalar_tensor_tensor", stt[:, 7:8],
                               stt[:, 0:1], 1.0 / D, stt[:, 6:7],
                               ALU.mult, ALU.mult)
                        pl.opf("vector", "tensor_scalar", xcb[:], xi,
                               stt[:, 6:7], stt[:, 7:8], ALU.mult, ALU.subtract)
                        xrot_rel[h] = pl.snap("dve")
                        pl.opf("vector", "tensor_tensor", t1b[:], xcb[:], nogb[:],
                               ALU.mult)
                        pl.op("vector", "tensor_tensor",
                              xn[:, i * 1024:(i + 1) * 1024], t1b[:], nobb[:],
                              ALU.add, inc="dve")
                        pl.wait("tensor", pl.snap("dve"))
                        tph = ln_tp[:, (i % 2) * 1024:((i % 2) + 1) * 1024]
                        for d in range(8):
                            pl.op("tensor", "transpose",
                                  tph[:, d * 128:(d + 1) * 128],
                                  xn[:, i * 1024 + d * 128:i * 1024 + (d + 1) * 128],
                                  idb_s[:], inc="pe" if d == 7 else None)
                        pl.wait("vector", pl.snap("pe"))
                        pl.op("vector", "tensor_copy",
                              _r3(xnT, 4096)[:, :, i * 128:(i + 1) * 128],
                              _r3(tph, 128), inc="dve")
                ln_done = pl.snap("dve")
                if dump and u == "A":
                    pl.wait("sync", ln_done)
                    pl.dma("sync", dbg["d_xn"], xn[:], "qs")
                    pl.dma("sync", dbg["d_xnT"], xnT[:], "qs")

                ring_rel = [None] * RING
                qg_slot_end = []
                slot_srcs = []
                for p in range(8):
                    eA, eB = 2 * p, 2 * p + 1
                    for fh in range(2):
                        for k in range(8):
                            slot_srcs.append((
                                wg[eA * 1024 + k * 128: eA * 1024 + (k + 1) * 128,
                                   fh * 1024:(fh + 1) * 1024],
                                wg[eB * 1024 + k * 128: eB * 1024 + (k + 1) * 128,
                                   fh * 1024:(fh + 1) * 1024]))
                    for fh in range(2):
                        for k in range(8):
                            slot_srcs.append((
                                wu[eA * 1024 + k * 128: eA * 1024 + (k + 1) * 128,
                                   fh * 1024:(fh + 1) * 1024],
                                wu[eB * 1024 + k * 128: eB * 1024 + (k + 1) * 128,
                                   fh * 1024:(fh + 1) * 1024]))
                    for fc in range(16):
                        slot_srcs.append((
                            wd[eA * 2048 + fc * 128: eA * 2048 + (fc + 1) * 128, :],
                            wd[eB * 2048 + fc * 128: eB * 2048 + (fc + 1) * 128, :]))

                def emit_slot_dma(j):
                    r = j % RING
                    if ring_rel[r] is not None:
                        pl.wait("gpsimd", ring_rel[r])
                    a_src, b_src = slot_srcs[j]
                    sl = ring[:, r * 2048: r * 2048 + 1024]
                    sr = ring[:, r * 2048 + 1024: (r + 1) * 2048]
                    pl.dma("gpsimd", sl, a_src, "qg")
                    pl.dma("gpsimd", sr, b_src, "qg")
                    qg_slot_end.append(pl.snap("qg"))

                def slot_ap(j):
                    r = j % RING
                    return ring[:, r * 2048:(r + 1) * 2048]

                for j in range(min(RING, len(slot_srcs))):
                    emit_slot_dma(j)
                next_dma = [RING]

                def consume_slot(j, passes_fn):
                    pl.wait("tensor", qg_slot_end[j])
                    passes_fn(slot_ap(j))
                    ring_rel[j % RING] = pl.snap("pe")
                    if next_dma[0] < len(slot_srcs):
                        emit_slot_dma(next_dma[0])
                        next_dma[0] += 1

                for m in range(8):
                    with nc.psum_tensor("lps%d" % m + u, [128, 4096], F32) as lps:
                        pl.wait("tensor", ln_done)
                        for d in range(8):
                            for nb in range(8):
                                pl.op("tensor", "matmul",
                                      lps[:, nb * 512:(nb + 1) * 512],
                                      seT[:, d * 1024 + m * 128:
                                          d * 1024 + (m + 1) * 128],
                                      xnT[:, d * 4096 + nb * 512:
                                          d * 4096 + (nb + 1) * 512],
                                      start=(d == 0), stop=(d == 7),
                                      inc="pe" if (d == 7 and nb == 7) else None)
                        pl.wait("vector", pl.snap("pe"))
                        pl.op("vector", "reduce_max", stt[:, 8:9], lps[:],
                              axis=AX.X, negate=True, inc="dve")
                        negmx = pl.snap("dve")
                        lps_bf = lps[:].bitcast(BF16)
                        stg_rel = [None, None]
                        for j in range(8):
                            hh = j % 2
                            pl.wait("scalar", negmx)
                            if stg_rel[hh] is not None:
                                pl.wait("scalar", stg_rel[hh])
                            pl.op("scalar", "activation",
                                  stg[:, hh * 512:(hh + 1) * 512],
                                  lps[:, j * 512:(j + 1) * 512], AF.Exp,
                                  bias=stt[:, 8:9], accum_out=ssc[:, j:j + 1],
                                  inc="act")
                            pl.wait("tensor", pl.snap("act"))
                            for q in range(4):
                                pl.op("tensor", "transpose",
                                      lps_bf[:, j * 1024 + q * 128:
                                             j * 1024 + (q + 1) * 128],
                                      stg[:, hh * 512 + q * 128:
                                          hh * 512 + (q + 1) * 128],
                                      idb_s[:], inc="pe" if q == 3 else None)
                            stg_rel[hh] = pl.snap("pe")
                            pl.wait("vector", pl.snap("pe"))
                            pl.op("vector", "tensor_copy",
                                  dT[:, j * 512:(j + 1) * 512],
                                  lps_bf[:, j * 1024: j * 1024 + 512], inc="dve")
                        pl.opf("vector", "reduce_sum", stt[:, 9:10], ssc[:, 0:8],
                               axis=AX.X)
                        pl.opf("vector", "reciprocal", rcpS[:, m:m + 1],
                               stt[:, 9:10])
                        pl.op("vector", "engine_nop", inc="dve")
                        dT_done = pl.snap("dve")
                        if dump and u == "A" and m == 0:
                            pl.wait("sync", dT_done)
                            pl.dma("sync", dbg["d_dT"], dT[:], "qs")
                            pl.dma("sync", dbg["d_stt"], stt[:], "qs")
                            pl.dma("sync", dbg["d_ssc"], ssc[:], "qs")
                    with nc.psum_tensor("sps%d" % m + u, [128, 1024], F32) as sps, \
                         nc.psum_tensor("stp%d" % m + u, [128, 1024], BF16) as stp:
                        pl.wait("tensor", dT_done)
                        for k in range(NT):
                            for nh in range(2):
                                pl.op("tensor", "matmul",
                                      sps[:, nh * 512:(nh + 1) * 512],
                                      dT[:, k * 128:(k + 1) * 128],
                                      xn[:, k * 1024 + nh * 512:
                                         k * 1024 + (nh + 1) * 512],
                                      start=(k == 0), stop=(k == NT - 1),
                                      inc="pe" if (k == NT - 1 and nh == 1) else None)
                        pl.wait("vector", pl.snap("pe"))
                        pl.op("vector", "tensor_scalar", slot_bf[:], sps[:],
                              rcpS[:, m:m + 1], None, ALU.mult, inc="dve")
                        pl.wait("tensor", pl.snap("dve"))
                        for d in range(8):
                            pl.op("tensor", "transpose",
                                  stp[:, d * 128:(d + 1) * 128],
                                  slot_bf[:, d * 128:(d + 1) * 128], idb_s[:],
                                  inc="pe" if d == 7 else None)
                        pl.wait("vector", pl.snap("pe"))
                        pl.op("vector", "tensor_copy",
                              _r3(hT, 1024)[:, :, m * 128:(m + 1) * 128],
                              _r3(stp, 128), inc="dve")
                        ln_done = pl.snap("dve")
                        if dump and u == "A" and m == 0:
                            pl.wait("sync", pl.snap("dve"))
                            pl.dma("sync", dbg["d_slot"], slot_bf[:], "qs")
                        if dump and u == "A" and m == 7:
                            pl.wait("sync", pl.snap("dve"))
                            pl.dma("sync", dbg["d_hT"], hT[:], "qs")

                pl.barrier()
                AS2.close()

                eout = FE(nc.sbuf_tensor("eout" + u, [128, ND * 1024], BF16))
                with nc.sbuf_tensor("gsb" + u, [128, 2048], BF16) as gsb, \
                     nc.sbuf_tensor("usb" + u, [128, 2048], BF16) as usb, \
                     nc.sbuf_tensor("gub" + u, [128, 2048], BF16) as gub, \
                     nc.sbuf_tensor("guT" + u, [128, 2048], BF16) as guT, \
                     nc.sbuf_tensor("bstgA" + u, [1, 5120], BF16) as bstgA, \
                     nc.sbuf_tensor("bstgB" + u, [1, 5120], BF16) as bstgB, \
                     nc.psum_tensor("gpA" + u, [128, 1024], F32) as gpA, \
                     nc.psum_tensor("gpB" + u, [128, 1024], F32) as gpB, \
                     nc.psum_tensor("gtp" + u, [128, 2048], BF16) as gtp:
                    jbase = 0
                    eout_rel = None
                    usb_rel = None
                    bias_rel = None
                    npairs = 0 if "noexp" in skip else (1 if "exp0" in skip else 8)
                    for p in range(npairs):
                        if "biasdma" in skip:
                            bias_ld = pl.snap("qg")
                        if bias_rel is not None and "biasdma" not in skip:
                            pl.wait("gpsimd", bias_rel)
                        if "biasdma" not in skip:
                            pl.dma("gpsimd", bstgA[0:1, 0:2048],
                                   bgE[0:1, p * 2048:(p + 1) * 2048], "qg")
                        if "biasdma" not in skip:
                            pl.dma("gpsimd", bstgB[0:1, 0:2048],
                                   bgE[1:2, p * 2048:(p + 1) * 2048], "qg")
                            pl.dma("gpsimd", bstgA[0:1, 2048:4096],
                                   buE[0:1, p * 2048:(p + 1) * 2048], "qg")
                            pl.dma("gpsimd", bstgB[0:1, 2048:4096],
                                   buE[1:2, p * 2048:(p + 1) * 2048], "qg")
                            pl.dma("gpsimd", bstgA[0:1, 4096:5120],
                                   bdE[0:1, p * 1024:(p + 1) * 1024], "qg")
                            pl.dma("gpsimd", bstgB[0:1, 4096:5120],
                                   bdE[1:2, p * 1024:(p + 1) * 1024], "qg")
                            bias_ld = pl.snap("qg")
                        mats = (("g", 0),) if ("gateonly" in skip) else \
                            (("g", 0), ("u", 2048))
                        for mat, boff in mats:
                            if mat == "g" and eout_rel is not None:
                                pl.wait("tensor", eout_rel)
                            if mat == "u":
                                pl.wait("tensor", gdone)
                            for fh in ((0,) if "fh0" in skip else (0, 1)):
                                if fh == 1:
                                    pl.wait("tensor",
                                            gdone if mat == "g" else usb_rel)
                                for k in range(8):
                                    j = jbase + fh * 8 + k

                                    def passes(sl, p=p, k=k):
                                        lh = hT[:, k * 1024 + p * 128:
                                                k * 1024 + (p + 1) * 128]
                                        for ab, pc in ((0, gpA), (1, gpB)):
                                            for nh in range(2):
                                                pl.op(
                                                    "tensor", "matmul",
                                                    pc[:, nh * 512:(nh + 1) * 512],
                                                    lh,
                                                    sl[:, ab * 1024 + nh * 512:
                                                       ab * 1024 + (nh + 1) * 512],
                                                    start=(k == 0), stop=False,
                                                    inc="pe" if (ab == 1 and
                                                                 nh == 1) else None)
                                    consume_slot(j, passes)
                                if "bias" not in skip:
                                    pl.wait("tensor", bias_ld)
                                    for ab, pc, bs in ((0, gpA, bstgA), (1, gpB, bstgB)):
                                        for nh in range(2):
                                            pl.op("tensor", "matmul",
                                                  pc[:, nh * 512:(nh + 1) * 512],
                                                  one1_s[0:1, :],
                                                  bs[0:1, boff + fh * 1024 + nh * 512:
                                                     boff + fh * 1024 + (nh + 1) * 512],
                                                  start=False, stop=True,
                                                  inc="pe" if (ab == 1 and
                                                               nh == 1) else None)
                                if mat == "g":
                                    if "nogelu" in skip:
                                        pl.op("scalar", "nop", inc="act")
                                        gdone = pl.snap("act")
                                    else:
                                        pl.wait("scalar", pl.snap("pe"))
                                        pl.op("scalar", "activation",
                                              gsb[0:64, fh * 1024:(fh + 1) * 1024],
                                              gpA[0:64, :], AF.Gelu)
                                        pl.op("scalar", "activation",
                                              gsb[64:128, fh * 1024:(fh + 1) * 1024],
                                              gpB[64:128, :], AF.Gelu, inc="act")
                                        gdone = pl.snap("act")
                                    if "gelubar" in skip:
                                        pl.barrier()
                                else:
                                    pl.wait("vector", pl.snap("pe"))
                                    pl.op("vector", "tensor_copy",
                                          usb[0:64, fh * 1024:(fh + 1) * 1024],
                                          gpA[0:64, :])
                                    pl.opf("vector", "tensor_copy",
                                           usb[64:128, fh * 1024:(fh + 1) * 1024],
                                           gpB[64:128, :])
                                    usb_rel = pl.snap("dve")
                            jbase += 16
                        if "gateonly" in skip:
                            jbase += 32
                            continue
                        pl.wait("vector", gdone)
                        pl.op("vector", "tensor_tensor", gub[:], gsb[:],
                              usb[:], ALU.mult, inc="dve")
                        pl.wait("tensor", pl.snap("dve"))
                        pl.wait("tensor", usb_rel)
                        for fc in range(16):
                            pl.op("tensor", "transpose",
                                  gtp[:, fc * 128:(fc + 1) * 128],
                                  gub[:, fc * 128:(fc + 1) * 128], idb_s[:],
                                  inc="pe" if fc == 15 else None)
                        pl.wait("vector", pl.snap("pe"))
                        pl.op("vector", "tensor_copy", guT[:], gtp[:], inc="dve")
                        pl.wait("tensor", pl.snap("dve"))
                        if eout_rel is not None:
                            pl.wait("tensor", eout_rel)
                        if "nodown" in skip:
                            jbase += 16
                            continue
                        for fc in range(16):
                            j = jbase + fc

                            def dpasses(sl, fc=fc):
                                lh = guT[:, fc * 128:(fc + 1) * 128]
                                for ab, pc in ((0, gpA), (1, gpB)):
                                    for nh in range(2):
                                        pl.op("tensor", "matmul",
                                              pc[:, nh * 512:(nh + 1) * 512],
                                              lh,
                                              sl[:, ab * 1024 + nh * 512:
                                                 ab * 1024 + (nh + 1) * 512],
                                              start=(fc == 0), stop=False,
                                              inc="pe" if (ab == 1 and
                                                           nh == 1) else None)
                            consume_slot(j, dpasses)
                        jbase += 16
                        if "bias" not in skip:
                            for ab, pc, bs in ((0, gpA, bstgA), (1, gpB, bstgB)):
                                for nh in range(2):
                                    pl.op("tensor", "matmul",
                                          pc[:, nh * 512:(nh + 1) * 512],
                                          one1_s[0:1, :],
                                          bs[0:1, 4096 + nh * 512:
                                             4096 + (nh + 1) * 512],
                                          start=False, stop=True,
                                          inc="pe" if (ab == 1 and nh == 1) else None)
                        bias_rel = pl.snap("pe")
                        pl.wait("vector", pl.snap("pe"))
                        pl.op("vector", "tensor_copy",
                              eout[0:64, p * 1024:(p + 1) * 1024], gpA[0:64, :])
                        pl.op("vector", "tensor_copy",
                              eout[64:128, p * 1024:(p + 1) * 1024],
                              gpB[64:128, :], inc="dve")
                        eout_rel = pl.snap("dve")
                if dump and u == "A" and eout_rel is not None:
                    pl.wait("sync", eout_rel)
                    pl.dma("sync", dbg["d_eout"], eout[:], "qs")
                with nc.sbuf_tensor("cstg" + u, [128, 1024], BF16) as cstg, \
                     nc.sbuf_tensor("cTs" + u, [128, 1024], BF16) as cTs, \
                     nc.sbuf_tensor("yrot" + u, [128, 2 * 1024], F32) as yrot, \
                     nc.psum_tensor("cps" + u, [128, 2 * 1024], F32) as cps, \
                     nc.psum_tensor("ctp" + u, [128, 1024], BF16) as ctp, \
                     nc.psum_tensor("ops" + u, [128, 1024], F32) as ops:
                    yst_rel = [None, None]
                    cstg_rel = None
                    cT_rel = None
                    ops_rel = None
                    if eout_rel is not None:
                        pl.wait("tensor", eout_rel)
                    for i in range(NT):
                        h = i % 2
                        if xrot_rel[h] is not None:
                            pl.wait("scalar", xrot_rel[h])
                        pl.dma("scalar", xrot[:, h * 1024:(h + 1) * 1024],
                               src[i * 128:(i + 1) * 128, :], "qc")
                        xls = pl.snap("qc")
                        cph = cps[:, h * 1024:(h + 1) * 1024]
                        for d in range(8):
                            for nh in range(2):
                                pl.op("tensor", "matmul",
                                      cph[:, nh * 512:(nh + 1) * 512],
                                      xnT[:, d * 4096 + i * 128:
                                          d * 4096 + (i + 1) * 128],
                                      seT[:, d * 1024 + nh * 512:
                                          d * 1024 + (nh + 1) * 512],
                                      start=(d == 0), stop=(d == 7),
                                      inc="pe" if (d == 7 and nh == 1) else None)
                        pl.wait("vector", pl.snap("pe"))
                        pl.op("vector", "reduce_max", stt[:, 10:11], cph,
                              axis=AX.X, negate=True, inc="dve")
                        pl.wait("scalar", pl.snap("dve"))
                        if cstg_rel is not None:
                            pl.wait("scalar", cstg_rel)
                        pl.op("scalar", "activation", cstg[:], cph, AF.Exp,
                              bias=stt[:, 10:11], accum_out=stt[:, 11:12],
                              inc="act")
                        pl.wait("vector", pl.snap("act"))
                        pl.opf("vector", "reciprocal", stt[:, 12:13], stt[:, 11:12])
                        pl.opf("vector", "tensor_scalar", stt[:, 13:14],
                               stt[:, 12:13], 0.5, None, ALU.mult)
                        pl.wait("tensor", pl.snap("act"))
                        if cT_rel is not None:
                            pl.wait("tensor", cT_rel)
                        for q in range(8):
                            pl.op("tensor", "transpose",
                                  ctp[:, q * 128:(q + 1) * 128],
                                  cstg[:, q * 128:(q + 1) * 128], idb_s[:],
                                  inc="pe" if q == 7 else None)
                        cstg_rel = pl.snap("pe")
                        pl.wait("vector", pl.snap("pe"))
                        pl.op("vector", "tensor_copy", cTs[:], ctp[:], inc="dve")
                        pl.wait("tensor", pl.snap("dve"))
                        if i == 0 and eout_rel is not None:
                            pl.wait("tensor", eout_rel)
                        if ops_rel is not None:
                            pl.wait("tensor", ops_rel)
                        for k in range(8):
                            for nh in range(2):
                                pl.op("tensor", "matmul",
                                      ops[:, nh * 512:(nh + 1) * 512],
                                      cTs[:, k * 128:(k + 1) * 128],
                                      eout[:, k * 1024 + nh * 512:
                                           k * 1024 + (nh + 1) * 512],
                                      start=(k == 0), stop=(k == 7),
                                      inc="pe" if (k == 7 and nh == 1) else None)
                        cT_rel = pl.snap("pe")
                        pl.wait("vector", pl.snap("pe"))
                        pl.wait("vector", xls)
                        if yst_rel[h] is not None:
                            pl.wait("vector", yst_rel[h])
                        pl.op("vector", "scalar_tensor_tensor",
                              yrot[:, h * 1024:(h + 1) * 1024], ops[:],
                              stt[:, 13:14], xrot[:, h * 1024:(h + 1) * 1024],
                              ALU.mult, ALU.add, inc="dve")
                        xrot_rel[h] = pl.snap("dve")
                        ops_rel = pl.snap("dve")
                        pl.wait("scalar", pl.snap("dve"))
                        pl.dma("scalar", dst[i * 128:(i + 1) * 128, :],
                               yrot[:, h * 1024:(h + 1) * 1024], "qc")
                        yst_rel[h] = pl.snap("qc")
            pl.barrier()

        def attn(src, dst):
            with ExitStack() as AS:
                AE = AS.enter_context
                wbf = AE(nc.sbuf_tensor("wbf", [128, 4 * 8192], BF16))
                lngb = AE(nc.sbuf_tensor("lngb", [128, 1024], BF16))
                lnbb = AE(nc.sbuf_tensor("lnbb", [128, 1024], BF16))
                xch = AE(nc.sbuf_tensor("xch", [128, 4 * 1024], F32))
                xa_stg = AE(nc.sbuf_tensor("xa_stg", [128, 1024], BF16))
                xaT = AE(nc.sbuf_tensor("xaT", [128, 4096], BF16))
                lastT = AE(nc.sbuf_tensor("lastT", [128, 512], BF16))
                dbuf = AE(nc.sbuf_tensor("dbuf", [128, 512], BF16))
                kinT = AE(nc.sbuf_tensor("kinT", [128, 4096], BF16))
                vinT = AE(nc.sbuf_tensor("vinT", [128, 4096], BF16))
                rinT = AE(nc.sbuf_tensor("rinT", [128, 4096], BF16))
                ekT = AE(nc.sbuf_tensor("ekT", [128, 4096], BF16))
                vT = AE(nc.sbuf_tensor("vT", [128, 4096], BF16))
                emT = AE(nc.sbuf_tensor("emT", [128, 4096], BF16))
                ubuf = AE(nc.sbuf_tensor("ubuf", [128, 512], F32))
                wbuf2 = AE(nc.sbuf_tensor("wbuf2", [128, 512], F32))
                rcb = AE(nc.sbuf_tensor("rcb", [128, 512], F32))
                numT = AE(nc.sbuf_tensor("numT", [128, 4096], F32))
                denT = AE(nc.sbuf_tensor("denT", [128, 4096], BF16))
                gatedT = AE(nc.sbuf_tensor("gatedT", [128, 4096], BF16))
                carry = AE(nc.sbuf_tensor("carry", [128, 8], BF16))
                snum = AE(nc.sbuf_tensor("snum", [128, 8], F32))
                sden = AE(nc.sbuf_tensor("sden", [128, 8], F32))
                stt2 = AE(nc.sbuf_tensor("stt2", [128, 16], F32))

                with nc.psum_tensor("bc2", [128, 1024], F32) as bc2:
                    bcast_vec(lng, lngb, bc2, "lng")
                    bcast_vec(lnb, lnbb, bc2, "lnb")
                for w_i, wsrc in enumerate((wkT, wvT, wrT, woT)):
                    for ck in range(8):
                        pl.dma("gpsimd",
                               wbf[:, w_i * 8192 + ck * 1024:
                                   w_i * 8192 + (ck + 1) * 1024],
                               wsrc[ck * 128:(ck + 1) * 128, :], "qg")
                wld = pl.snap("qg")
                pl.op("vector", "memset", carry[:], 0.0)
                pl.op("vector", "memset", snum[:], 0.0)
                pl.op("vector", "memset", sden[:], 0.0, inc="dve")
                init_done = pl.snap("dve")
                pl.wait("tensor", wld)
                pl.wait("tensor", init_done)

                xch_rel = None
                yst_rel = [None, None]
                for ch in range(NCH):
                    if xch_rel is not None:
                        pl.wait("scalar", xch_rel)
                    for tsub in range(4):
                        pl.dma("scalar", xch[:, tsub * 1024:(tsub + 1) * 1024],
                               src[(ch * 4 + tsub) * 128:
                                   (ch * 4 + tsub + 1) * 128, :], "qc")
                    xls = pl.snap("qc")
                    pl.wait("vector", xls)
                    pl.wait("scalar", xls)
                    if xch_rel is not None:
                        pl.wait("tensor", xch_rel)
                    with nc.psum_tensor("atp%d" % ch, [128, 2 * 1024], BF16) as atp, \
                         nc.psum_tensor("sqa%d" % ch, [128, 1024], F32) as sqa:
                        for tsub in range(4):
                            xi = xch[:, tsub * 1024:(tsub + 1) * 1024]
                            hh = tsub % 2
                            pl.opf("vector", "reduce_sum", stt2[:, 0:1], xi,
                                   axis=AX.X)
                            s1d = pl.snap("dve")
                            pl.op("scalar", "activation", sqa[:, 0:1024], xi,
                                  AF.Square, accum_out=stt2[:, 1:2])
                            pl.wait("scalar", s1d)
                            pl.op("scalar", "activation", stt2[:, 4:5],
                                  stt2[:, 0:1], AF.Square,
                                  scale=(1.0 + EPS) ** 0.5 / D)
                            pl.op("scalar", "activation", stt2[:, 5:6],
                                  stt2[:, 1:2], AF.Copy,
                                  scale=(1.0 + EPS) / D, bias=EPS * EPS,
                                  inc="act")
                            pl.wait("vector", pl.snap("act"))
                            pl.opf("vector", "tensor_tensor", stt2[:, 5:6],
                                   stt2[:, 5:6], stt2[:, 4:5], ALU.subtract)
                            pl.wait("scalar", pl.snap("dve"))
                            pl.op("scalar", "activation", stt2[:, 6:7],
                                  stt2[:, 5:6], AF.Ln)
                            pl.op("scalar", "activation", stt2[:, 6:7],
                                  stt2[:, 6:7], AF.Exp, scale=-0.5, inc="act")
                            pl.wait("vector", pl.snap("act"))
                            pl.opf("vector", "scalar_tensor_tensor", stt2[:, 7:8],
                                   stt2[:, 0:1], 1.0 / D, stt2[:, 6:7],
                                   ALU.mult, ALU.mult)
                            pl.opf("vector", "tensor_scalar", xa_stg[:], xi,
                                   stt2[:, 6:7], stt2[:, 7:8],
                                   ALU.mult, ALU.subtract)
                            pl.opf("vector", "tensor_tensor", xa_stg[:],
                                   xa_stg[:], lngb[:], ALU.mult)
                            pl.op("vector", "tensor_tensor", xa_stg[:],
                                  xa_stg[:], lnbb[:], ALU.add, inc="dve")
                            pl.wait("tensor", pl.snap("dve"))
                            tph = atp[:, hh * 1024:(hh + 1) * 1024]
                            for c in range(8):
                                pl.op("tensor", "transpose",
                                      tph[:, c * 128:(c + 1) * 128],
                                      xa_stg[:, c * 128:(c + 1) * 128],
                                      idb_s[:], inc="pe" if c == 7 else None)
                            pl.wait("vector", pl.snap("pe"))
                            pl.op("vector", "tensor_copy",
                                  _r3(xaT, 512)[:, :, tsub * 128:(tsub + 1) * 128],
                                  _r3(tph, 128), inc="dve")
                    for c in range(8):
                        xc_ap = xaT[:, c * 512:(c + 1) * 512]
                        pl.op("vector", "tensor_copy", lastT[:, 0:1],
                              carry[:, c:c + 1])
                        pl.opf("vector", "tensor_copy", lastT[:, 1:512],
                               xaT[:, c * 512: c * 512 + 511])
                        pl.op("vector", "tensor_copy", carry[:, c:c + 1],
                              xaT[:, c * 512 + 511: c * 512 + 512])
                        pl.opf("vector", "tensor_tensor", dbuf[:], xc_ap,
                               lastT[:], ALU.subtract)
                        pl.op("vector", "scalar_tensor_tensor",
                              kinT[:, c * 512:(c + 1) * 512], dbuf[:],
                              tmk_s[:, c:c + 1], lastT[:], ALU.mult, ALU.add)
                        pl.op("vector", "scalar_tensor_tensor",
                              vinT[:, c * 512:(c + 1) * 512], dbuf[:],
                              tmv_s[:, c:c + 1], lastT[:], ALU.mult, ALU.add)
                        pl.op("vector", "scalar_tensor_tensor",
                              rinT[:, c * 512:(c + 1) * 512], dbuf[:],
                              tmr_s[:, c:c + 1], lastT[:], ALU.mult, ALU.add,
                              inc="dve")
                    mix_done = pl.snap("dve")
                    with nc.psum_tensor("prj%d" % ch, [128, 4096], F32) as prj:
                        pl.wait("tensor", mix_done)
                        ev_snap = None

                        def proj(w_i, src_t, ev_fn):
                            nonlocal ev_snap
                            for mo in range(8):
                                if ev_snap is not None:
                                    pl.wait("tensor", ev_snap[mo])
                                for kc in range(8):
                                    pl.op("tensor", "matmul",
                                          prj[:, mo * 512:(mo + 1) * 512],
                                          wbf[:, w_i * 8192 + kc * 1024 + mo * 128:
                                              w_i * 8192 + kc * 1024 + (mo + 1) * 128],
                                          src_t[:, kc * 512:(kc + 1) * 512],
                                          start=(kc == 0), stop=(kc == 7),
                                          inc="pe" if kc == 7 else None)
                                ev_fn(mo, pl.snap("pe"))

                        ek_evs = []

                        def ek_ev(mo, s):
                            pl.wait("scalar", s)
                            pl.op("scalar", "activation",
                                  ekT[:, mo * 512:(mo + 1) * 512],
                                  prj[:, mo * 512:(mo + 1) * 512], AF.Exp,
                                  inc="act")
                            ek_evs.append(pl.snap("act"))
                        proj(0, kinT, ek_ev)
                        ev_snap = ek_evs

                        v_evs = []

                        def v_ev(mo, s):
                            pl.wait("vector", s)
                            pl.op("vector", "tensor_copy",
                                  vT[:, mo * 512:(mo + 1) * 512],
                                  prj[:, mo * 512:(mo + 1) * 512], inc="dve")
                            v_evs.append(pl.snap("dve"))
                        proj(1, vinT, v_ev)
                        ev_snap = v_evs

                        r_evs = []

                        def r_ev(mo, s):
                            pl.wait("scalar", s)
                            pl.op("scalar", "activation",
                                  emT[:, mo * 512:(mo + 1) * 512],
                                  prj[:, mo * 512:(mo + 1) * 512], AF.Exp,
                                  scale=-1.0, inc="act")
                            r_evs.append(pl.snap("act"))
                        proj(2, rinT, r_ev)
                        r_done = pl.snap("act")

                        for c in range(8):
                            pl.wait("vector", v_evs[c])
                            td_b = tdE[:, c:c + 1].broadcast_to([128, 512])
                            pl.opf("vector", "tensor_tensor", ubuf[:],
                                   ekT[:, c * 512:(c + 1) * 512],
                                   vT[:, c * 512:(c + 1) * 512], ALU.mult)
                            pl.op("vector", "tensor_tensor_scan",
                                  numT[:, c * 512:(c + 1) * 512], td_b,
                                  ubuf[:], snum[:, c:c + 1], ALU.mult, ALU.add)
                            pl.opf("vector", "tensor_tensor_scan",
                                   denT[:, c * 512:(c + 1) * 512], td_b,
                                   ekT[:, c * 512:(c + 1) * 512],
                                   sden[:, c:c + 1],
                                   ALU.mult, ALU.add)
                            pl.op("vector", "tensor_copy", snum[:, c:c + 1],
                                  numT[:, c * 512 + 511: c * 512 + 512])
                            pl.opf("vector", "tensor_copy", sden[:, c:c + 1],
                                   denT[:, c * 512 + 511: c * 512 + 512])
                            pl.opf("vector", "tensor_scalar",
                                   denT[:, c * 512:(c + 1) * 512],
                                   denT[:, c * 512:(c + 1) * 512],
                                   1e-8, None, ALU.add)
                            pl.opf("vector", "reciprocal", rcb[:],
                                   denT[:, c * 512:(c + 1) * 512])
                            pl.opf("vector", "tensor_tensor", wbuf2[:],
                                   numT[:, c * 512:(c + 1) * 512],
                                   rcb[:], ALU.mult)
                            if c == 0:
                                pl.wait("vector", r_done)
                                pl.opf("vector", "tensor_scalar", emT[:],
                                       emT[:], 1.0, None, ALU.add)
                            pl.opf("vector", "reciprocal", rcb[:],
                                   emT[:, c * 512:(c + 1) * 512])
                            pl.op("vector", "tensor_tensor",
                                  gatedT[:, c * 512:(c + 1) * 512], wbuf2[:],
                                  rcb[:], ALU.mult,
                                  inc="dve")
                        gated_done = pl.snap("dve")
                    with nc.psum_tensor("ops2%d" % ch, [128, 2 * 1024], F32) as ops2:
                        pl.wait("tensor", gated_done)
                        oph_rel = [None, None]
                        for tsub in range(4):
                            hh = tsub % 2
                            oph = ops2[:, hh * 1024:(hh + 1) * 1024]
                            if oph_rel[hh] is not None:
                                pl.wait("tensor", oph_rel[hh])
                            for c in range(8):
                                for nh in range(2):
                                    pl.op("tensor", "matmul",
                                          oph[:, nh * 512:(nh + 1) * 512],
                                          gatedT[:, c * 512 + tsub * 128:
                                                 c * 512 + (tsub + 1) * 128],
                                          wbf[:, 3 * 8192 + c * 1024 + nh * 512:
                                              3 * 8192 + c * 1024 + (nh + 1) * 512],
                                          start=(c == 0), stop=(c == 7),
                                          inc="pe" if (c == 7 and nh == 1) else None)
                            pl.wait("vector", pl.snap("pe"))
                            xt = xch[:, tsub * 1024:(tsub + 1) * 1024]
                            pl.op("vector", "tensor_tensor", xt, oph, xt,
                                  ALU.add, inc="dve")
                            oph_rel[hh] = pl.snap("dve")
                            pl.wait("scalar", pl.snap("dve"))
                            pl.dma("scalar",
                                   dst[(ch * 4 + tsub) * 128:
                                       (ch * 4 + tsub + 1) * 128, :],
                                   xt, "qc")
                        xch_rel = pl.snap("qc")
            pl.barrier()

        ffn(x_in, y if dbg_stage == 1 else xres, "A")
        if dbg_stage >= 2:
            attn(xres, y if dbg_stage == 2 else xres)
        if dbg_stage >= 3:
            ffn(xres, y, "B")

        pl.wait("scalar", pl.snap("qc"))
        pl.wait("gpsimd", pl.snap("qg"))

        with nc.Block() as block:
            @block.sync
            def _(eng):
                for f in pl.streams["sync"]:
                    f(eng)

            @block.tensor
            def _(eng):
                for f in pl.streams["tensor"]:
                    f(eng)

            @block.vector
            def _(eng):
                for f in pl.streams["vector"]:
                    f(eng)

            @block.scalar
            def _(eng):
                for f in pl.streams["scalar"]:
                    f(eng)

            @block.gpsimd
            def _(eng):
                for f in pl.streams["gpsimd"]:
                    f(eng)

    return nc


_NC_CACHE = {}


def _get_nc(dbg_stage=3, dump=False, paranoid=False, skip=()):
    key = (dbg_stage, dump, paranoid, tuple(skip))
    if key not in _NC_CACHE:
        _NC_CACHE[key] = build(dbg_stage, dump, paranoid, skip)
    return _NC_CACHE[key]


def kernel(L0_attn_x, L0_attn_time_decay, L0_attn_time_mix_k, L0_attn_time_mix_v,
           L0_attn_time_mix_r, L0_attn_Wk, L0_attn_Wv, L0_attn_Wr, L0_attn_Wo,
           L0_attn_ln_g, L0_attn_ln_b, L0_ffn_norm_g, L0_ffn_norm_b, L0_ffn_slot_g,
           L0_ffn_slot_b, L0_ffn_scale, L0_ffn_slot_embeds, L0_ffn_Wg, L0_ffn_bg,
           L0_ffn_Wu, L0_ffn_bu, L0_ffn_Wd, L0_ffn_bd, dbg_stage=3, dump=False,
           paranoid=False, skip=()):
    f32 = np.float32
    C = np.ascontiguousarray

    def vecP(v):
        return C(np.asarray(v, f32).reshape(8, 128).T)

    def rowE(b, w):
        a = np.asarray(b, f32).reshape(8, 2, w).transpose(1, 0, 2).reshape(2, 8 * w)
        return C(a)

    common = {
        "wkT": C(np.asarray(L0_attn_Wk, f32).T),
        "wvT": C(np.asarray(L0_attn_Wv, f32).T),
        "wrT": C(np.asarray(L0_attn_Wr, f32).T),
        "woT": C(np.asarray(L0_attn_Wo, f32).T),
        "wg": C(np.asarray(L0_ffn_Wg, f32).reshape(NE * D, FF)),
        "wu": C(np.asarray(L0_ffn_Wu, f32).reshape(NE * D, FF)),
        "wd": C(np.asarray(L0_ffn_Wd, f32).reshape(NE * FF, D)),
        "semb": C(np.asarray(L0_ffn_slot_embeds, f32).reshape(1024, D)),
        "bgE": rowE(L0_ffn_bg, FF),
        "buE": rowE(L0_ffn_bu, FF),
        "bdE": rowE(L0_ffn_bd, D),
        "tdP": vecP(L0_attn_time_decay),
        "tmkP": vecP(L0_attn_time_mix_k),
        "tmvP": vecP(L0_attn_time_mix_v),
        "tmrP": vecP(L0_attn_time_mix_r),
        "lng": C(np.asarray(L0_attn_ln_g, f32).reshape(1, D)),
        "lnb": C(np.asarray(L0_attn_ln_b, f32).reshape(1, D)),
        "nog": C(np.asarray(L0_ffn_norm_g, f32).reshape(1, D)),
        "nob": C(np.asarray(L0_ffn_norm_b, f32).reshape(1, D)),
        "slg": C(np.asarray(L0_ffn_slot_g, f32).reshape(1, D)),
        "slb": C(np.asarray(L0_ffn_slot_b, f32).reshape(1, D)),
        "scl": C(np.asarray(L0_ffn_scale, f32).reshape(1, 1)),
        "idb": np.eye(128, dtype=bf),
        "e2": np.concatenate([
            np.concatenate([np.ones((1, 64)), np.zeros((1, 64))], 1),
            np.concatenate([np.zeros((1, 64)), np.ones((1, 64))], 1)],
            0).astype(bf),
        "one1": np.ones((1, 128), dtype=bf),
    }
    xs = np.asarray(L0_attn_x, f32)
    in_maps = [dict(common, x=C(xs[c])) for c in range(8)]
    nc = _get_nc(dbg_stage, dump, paranoid, skip)
    res = run_bass_kernel_spmd(nc, in_maps, core_ids=list(range(8)))
    out = np.stack([res.results[c]["y"] for c in range(8)], axis=0)
    if dump:
        return out.astype(np.float32), res.results
    return out.astype(np.float32)


# version 02
# speedup vs baseline: 1.0272x; 1.0272x over previous
import numpy as np
import ml_dtypes
from contextlib import ExitStack

import concourse.bass as bass
import concourse.mybir as mybir
from concourse.bass_utils import run_bass_kernel_spmd

F32 = mybir.dt.float32
BF16 = mybir.dt.bfloat16
AF = mybir.ActivationFunctionType
ALU = mybir.AluOpType
AX = mybir.AxisListType
bf = ml_dtypes.bfloat16

T, D = 4096, 1024
NT = 32
ND = 8
NE, SS, FF = 16, 64, 2048
EPS = 1e-5
RING = 3
CHUNK = 512
NCH = T // CHUNK

ENGS = ("sync", "tensor", "vector", "scalar", "gpsimd")


class Plan:
    def __init__(self, nc, stack):
        self.nc = nc
        self.streams = {e: [] for e in ENGS}
        self.sems = {}
        for name in ("qs", "qc", "qg", "pe", "dve", "act", "bar"):
            h = stack.enter_context(nc.semaphore(name))
            self.sems[name] = [h, 0]

    def emit(self, eng, fn):
        self.streams[eng].append(fn)

    def op(self, eng, meth, *args, inc=None, incn=1, **kw):
        if inc is not None:
            s = self.sems[inc]
            s[1] += incn
            h = s[0]
        else:
            h = None

        def f(e, m=meth, a=args, k=kw, h=h, n=incn):
            r = getattr(e, m)(*a, **k)
            if h is not None:
                r.then_inc(h, n)
        self.emit(eng, f)

    def snap(self, sem):
        return (sem, self.sems[sem][1])

    def wait(self, eng, snapshot):
        sem, v = snapshot
        if v <= 0:
            return
        h = self.sems[sem][0]
        self.emit(eng, lambda e, h=h, v=v: e.wait_ge(h, v))

    ENG_SEM = {"tensor": "pe", "vector": "dve", "scalar": "act"}

    def opf(self, eng, meth, *args, **kw):
        sem = self.ENG_SEM[eng]
        self.op(eng, meth, *args, inc=sem, **kw)
        self.wait(eng, self.snap(sem))

    def dma(self, eng, out, in_, q):
        self.op(eng, "dma_start", inc=q, incn=16, out=out, in_=in_)

    def barrier(self):
        s = self.sems["bar"]
        for eng in ENGS:
            self.emit(eng, lambda e, h=s[0]: e.sem_inc(h, 1))
        s[1] += len(ENGS)
        v = s[1]
        for eng in ENGS:
            self.emit(eng, lambda e, h=s[0], v=v: e.wait_ge(h, v))


def _r3(ap, c):
    return ap[:].rearrange("p (a b) -> p a b", b=c)


def build(dbg_stage=3, dump=False, paranoid=False, skip=()):
    nc = bass.Bass("TRN2", target_bir_lowering=False)
    dt_in = {}

    def din(name, shape, dtype=F32):
        t = nc.dram_tensor(name, shape, dtype, kind="ExternalInput")
        dt_in[name] = t.ap()
        return dt_in[name]

    x_in = din("x", [T, D])
    wkT = din("wkT", [D, D]); wvT = din("wvT", [D, D])
    wrT = din("wrT", [D, D]); woT = din("woT", [D, D])
    wg = din("wg", [NE * D, FF]); wu = din("wu", [NE * D, FF])
    wd = din("wd", [NE * FF, D])
    semb = din("semb", [1024, D])
    bgE = din("bgE", [2, 8 * FF]); buE = din("buE", [2, 8 * FF])
    bdE = din("bdE", [2, 8 * D])
    tdP = din("tdP", [128, 8]); tmkP = din("tmkP", [128, 8])
    tmvP = din("tmvP", [128, 8]); tmrP = din("tmrP", [128, 8])
    lng = din("lng", [1, D]); lnb = din("lnb", [1, D])
    nog = din("nog", [1, D]); nob = din("nob", [1, D])
    slg = din("slg", [1, D]); slb = din("slb", [1, D])
    scl = din("scl", [1, 1])
    idb = din("idb", [128, 128], BF16)
    e2d = din("e2", [2, 128], BF16)
    one1 = din("one1", [1, 128], BF16)

    xres = nc.dram_tensor("xres", [T, D], F32).ap()
    y = nc.dram_tensor("y", [T, D], F32, kind="ExternalOutput").ap()
    dbg = {}
    if dump:
        for nm, sh, dt in (("d_xn", [128, NT * 1024], BF16),
                           ("d_xnT", [128, ND * 4096], BF16),
                           ("d_dT", [128, 4096], BF16),
                           ("d_stt", [128, 16], F32),
                           ("d_ssc", [128, 8], F32),
                           ("d_slot", [128, 1024], BF16),
                           ("d_hT", [128, ND * 1024], BF16),
                           ("d_eout", [128, ND * 1024], BF16),
                           ("d_gate", [128, 2048], BF16),
                           ("d_up", [128, 2048], BF16),
                           ("d_gu", [128, 2048], BF16),
                           ("d_guT", [128, 2048], BF16),
                           ("d_seT", [128, ND * 1024], BF16)):
            dbg[nm] = nc.dram_tensor(nm, sh, dt, kind="ExternalOutput").ap()

    with ExitStack() as S:
        E = S.enter_context
        pl = Plan(nc, S)

        idb_s = E(nc.sbuf_tensor("idb_s", [128, 128], BF16))
        e2_s = E(nc.sbuf_tensor("e2_s", [2, 128], BF16))
        one1_s = E(nc.sbuf_tensor("one1_s", [1, 128], BF16))
        seT = E(nc.sbuf_tensor("seT", [128, ND * 1024], BF16))
        tdE = E(nc.sbuf_tensor("tdE", [128, 8], F32))
        tmk_s = E(nc.sbuf_tensor("tmk_s", [128, 8], F32))
        tmv_s = E(nc.sbuf_tensor("tmv_s", [128, 8], F32))
        tmr_s = E(nc.sbuf_tensor("tmr_s", [128, 8], F32))
        vrow = E(nc.sbuf_tensor("vrow", [1, D], F32))
        vrow_b = E(nc.sbuf_tensor("vrow_b", [1, D], BF16))
        sclb = E(nc.sbuf_tensor("sclb", [128, 1], F32))

        pl.dma("sync", idb_s[:], idb, "qs")
        pl.dma("sync", e2_s[:], e2d, "qs")
        pl.dma("sync", one1_s[:], one1, "qs")
        pl.dma("sync", tdE[:], tdP, "qs")
        pl.dma("sync", tmk_s[:], tmkP, "qs")
        pl.dma("sync", tmv_s[:], tmvP, "qs")
        pl.dma("sync", tmr_s[:], tmrP, "qs")
        ld0 = pl.snap("qs")
        pl.wait("scalar", ld0)
        pl.op("scalar", "activation", tdE[:], tdE[:], AF.Exp, inc="act")

        vrow_rel = [None]

        def bcast_vec(vec_dram, dst_bf, ps_pool, tag):
            if vrow_rel[0] is not None:
                pl.wait("sync", vrow_rel[0])
            pl.dma("sync", vrow[:], vec_dram, "qs")
            pl.wait("vector", pl.snap("qs"))
            pl.wait("vector", pl.snap("pe"))
            pl.op("vector", "tensor_copy", vrow_b[:], vrow[:], inc="dve")
            vrow_rel[0] = pl.snap("dve")
            pl.wait("tensor", pl.snap("dve"))
            pl.wait("tensor", ld0)
            for nb in range(2):
                pl.op("tensor", "matmul", ps_pool[:, nb * 512:(nb + 1) * 512],
                      one1_s[:], vrow_b[:, nb * 512:(nb + 1) * 512],
                      start=True, stop=True, inc="pe" if nb == 1 else None)
            pl.wait("vector", pl.snap("pe"))
            pl.op("vector", "tensor_copy", dst_bf[:], ps_pool[:], inc="dve")

        def bcast_scl(ps_pool):
            if vrow_rel[0] is not None:
                pl.wait("sync", vrow_rel[0])
            pl.dma("sync", vrow[:, 0:1], scl, "qs")
            pl.wait("vector", pl.snap("qs"))
            pl.wait("vector", pl.snap("pe"))
            pl.op("vector", "tensor_copy", vrow_b[:, 0:1], vrow[:, 0:1], inc="dve")
            vrow_rel[0] = pl.snap("dve")
            pl.wait("tensor", pl.snap("dve"))
            pl.op("tensor", "matmul", ps_pool[:, 0:1], one1_s[:], vrow_b[:, 0:1],
                  start=True, stop=True, inc="pe")
            pl.wait("vector", pl.snap("pe"))
            pl.op("vector", "tensor_copy", sclb[:], ps_pool[:, 0:1], inc="dve")

        with nc.sbuf_tensor("se_x", [128, 1024], F32) as se_x, \
             nc.sbuf_tensor("se_sq", [128, 1024], F32) as se_sq, \
             nc.sbuf_tensor("se_bf", [128, 1024], BF16) as se_bf, \
             nc.sbuf_tensor("se_t1", [128, 1024], BF16) as se_t1, \
             nc.sbuf_tensor("slgb", [128, 1024], BF16) as slgb, \
             nc.sbuf_tensor("slbb", [128, 1024], BF16) as slbb, \
             nc.sbuf_tensor("st8", [128, 8], F32) as st8, \
             nc.psum_tensor("pre_ps", [128, 1024], F32) as pre_ps, \
             nc.psum_tensor("pre_tp", [128, 1024], BF16) as pre_tp:
            bcast_vec(slg, slgb, pre_ps, "slg")
            bcast_vec(slb, slbb, pre_ps, "slb")
            bcast_scl(pre_ps)
            for st in range(8):
                pl.dma("sync", se_x[:], semb[st * 128:(st + 1) * 128, :], "qs")
                lsn = pl.snap("qs")
                pl.wait("vector", lsn)
                pl.wait("scalar", lsn)
                pl.opf("vector", "reduce_sum", st8[:, 0:1], se_x[:], axis=AX.X)
                s1_done = pl.snap("dve")
                pl.op("scalar", "activation", se_sq[:], se_x[:], AF.Square,
                      accum_out=st8[:, 1:2])
                pl.wait("scalar", s1_done)
                pl.op("scalar", "activation", st8[:, 4:5], st8[:, 0:1],
                      AF.Square, scale=1.0 / D)
                pl.op("scalar", "activation", st8[:, 5:6], st8[:, 1:2],
                      AF.Copy, scale=1.0 / D, bias=EPS, inc="act")
                pl.wait("vector", pl.snap("act"))
                pl.opf("vector", "tensor_tensor", st8[:, 5:6], st8[:, 5:6],
                       st8[:, 4:5], ALU.subtract)
                pl.wait("scalar", pl.snap("dve"))
                pl.op("scalar", "activation", st8[:, 6:7], st8[:, 5:6], AF.Ln)
                pl.op("scalar", "activation", st8[:, 6:7], st8[:, 6:7], AF.Exp,
                      scale=-0.5, inc="act")
                pl.wait("vector", pl.snap("act"))
                pl.opf("vector", "scalar_tensor_tensor", st8[:, 7:8], st8[:, 0:1],
                       1.0 / D, st8[:, 6:7], ALU.mult, ALU.mult)
                pl.opf("vector", "tensor_scalar", se_bf[:], se_x[:],
                       st8[:, 6:7], st8[:, 7:8], ALU.mult, ALU.subtract)
                pl.opf("vector", "tensor_tensor", se_t1[:], se_bf[:], slgb[:], ALU.mult)
                pl.opf("vector", "tensor_tensor", se_t1[:], se_t1[:], slbb[:], ALU.add)
                pl.op("vector", "tensor_scalar", se_t1[:], se_t1[:],
                      sclb[:, 0:1], None, ALU.mult, inc="dve")
                pl.wait("tensor", pl.snap("dve"))
                for d in range(8):
                    pl.op("tensor", "transpose", pre_tp[:, d * 128:(d + 1) * 128],
                          se_t1[:, d * 128:(d + 1) * 128], idb_s[:],
                          inc="pe" if d == 7 else None)
                pl.wait("vector", pl.snap("pe"))
                pl.op("vector", "tensor_copy",
                      _r3(seT, 1024)[:, :, st * 128:(st + 1) * 128],
                      _r3(pre_tp, 128), inc="dve")
                pl.barrier()
            pl.barrier()
        if dump:
            pl.dma("sync", dbg["d_seT"], seT[:], "qs")

        def ffn(src, dst, u):
            with ExitStack() as FS:
                FE = FS.enter_context
                xnT = FE(nc.sbuf_tensor("xnT" + u, [128, ND * 4096], BF16))
                hT = FE(nc.sbuf_tensor("hT" + u, [128, ND * 1024], BF16))
                xrot = FE(nc.sbuf_tensor("xrot" + u, [128, 2 * 1024], F32))
                stt = FE(nc.sbuf_tensor("stt" + u, [128, 16], F32))
                ssc = FE(nc.sbuf_tensor("ssc" + u, [128, 8], F32))
                rcpS = FE(nc.sbuf_tensor("rcpS" + u, [128, 8], F32))
                ring = FE(nc.sbuf_tensor("ring" + u, [128, RING * 2048], BF16))
                AS2 = FE(ExitStack())
                AE2 = AS2.enter_context
                xn = AE2(nc.sbuf_tensor("xn" + u, [128, NT * 1024], BF16))
                dT = AE2(nc.sbuf_tensor("dT" + u, [128, 4096], BF16))
                stg = AE2(nc.sbuf_tensor("stg" + u, [128, 2 * 512], BF16))
                nogb = AE2(nc.sbuf_tensor("nogb" + u, [128, 1024], BF16))
                nobb = AE2(nc.sbuf_tensor("nobb" + u, [128, 1024], BF16))
                xcb = AE2(nc.sbuf_tensor("xcb" + u, [128, 1024], BF16))
                t1b = AE2(nc.sbuf_tensor("t1b" + u, [128, 1024], BF16))
                slot_bf = AE2(nc.sbuf_tensor("slot_bf" + u, [128, 1024], BF16))

                with nc.psum_tensor("bc_ps" + u, [128, 1024], F32) as bc_ps:
                    bcast_vec(nog, nogb, bc_ps, "nog")
                    bcast_vec(nob, nobb, bc_ps, "nob")

                xrot_rel = [None, None]

                with nc.psum_tensor("ln_tp" + u, [128, 2 * 1024], BF16) as ln_tp, \
                     nc.psum_tensor("sqp" + u, [128, 1024], F32) as sq:
                    for i in range(NT):
                        h = i % 2
                        if xrot_rel[h] is not None:
                            pl.wait("scalar", xrot_rel[h])
                        pl.dma("scalar", xrot[:, h * 1024:(h + 1) * 1024],
                               src[i * 128:(i + 1) * 128, :], "qc")
                        ls = pl.snap("qc")
                        xi = xrot[:, h * 1024:(h + 1) * 1024]
                        pl.wait("vector", ls)
                        pl.wait("scalar", ls)
                        pl.opf("vector", "reduce_sum", stt[:, 0:1], xi, axis=AX.X)
                        s1d = pl.snap("dve")
                        pl.op("scalar", "activation", sq[:, 0:1024], xi, AF.Square,
                              accum_out=stt[:, 1:2])
                        pl.wait("scalar", s1d)
                        pl.op("scalar", "activation", stt[:, 4:5], stt[:, 0:1],
                              AF.Square, scale=(1.0 + EPS) ** 0.5 / D)
                        pl.op("scalar", "activation", stt[:, 5:6], stt[:, 1:2],
                              AF.Copy, scale=(1.0 + EPS) / D, bias=EPS * EPS,
                              inc="act")
                        pl.wait("vector", pl.snap("act"))
                        pl.opf("vector", "tensor_tensor", stt[:, 5:6], stt[:, 5:6],
                               stt[:, 4:5], ALU.subtract)
                        pl.wait("scalar", pl.snap("dve"))
                        pl.op("scalar", "activation", stt[:, 6:7], stt[:, 5:6], AF.Ln)
                        pl.op("scalar", "activation", stt[:, 6:7], stt[:, 6:7],
                              AF.Exp, scale=-0.5, inc="act")
                        pl.wait("vector", pl.snap("act"))
                        pl.opf("vector", "scalar_tensor_tensor", stt[:, 7:8],
                               stt[:, 0:1], 1.0 / D, stt[:, 6:7],
                               ALU.mult, ALU.mult)
                        pl.opf("vector", "tensor_scalar", xcb[:], xi,
                               stt[:, 6:7], stt[:, 7:8], ALU.mult, ALU.subtract)
                        xrot_rel[h] = pl.snap("dve")
                        pl.opf("vector", "tensor_tensor", t1b[:], xcb[:], nogb[:],
                               ALU.mult)
                        pl.op("vector", "tensor_tensor",
                              xn[:, i * 1024:(i + 1) * 1024], t1b[:], nobb[:],
                              ALU.add, inc="dve")
                        pl.wait("tensor", pl.snap("dve"))
                        tph = ln_tp[:, (i % 2) * 1024:((i % 2) + 1) * 1024]
                        for d in range(8):
                            pl.op("tensor", "transpose",
                                  tph[:, d * 128:(d + 1) * 128],
                                  xn[:, i * 1024 + d * 128:i * 1024 + (d + 1) * 128],
                                  idb_s[:], inc="pe" if d == 7 else None)
                        pl.wait("vector", pl.snap("pe"))
                        pl.op("vector", "tensor_copy",
                              _r3(xnT, 4096)[:, :, i * 128:(i + 1) * 128],
                              _r3(tph, 128), inc="dve")
                ln_done = pl.snap("dve")
                if dump and u == "A":
                    pl.wait("sync", ln_done)
                    pl.dma("sync", dbg["d_xn"], xn[:], "qs")
                    pl.dma("sync", dbg["d_xnT"], xnT[:], "qs")

                ring_rel = [None] * RING
                qg_slot_end = []
                slot_srcs = []
                for p in range(8):
                    eA, eB = 2 * p, 2 * p + 1
                    for fh in range(2):
                        for k in range(8):
                            slot_srcs.append((
                                wg[eA * 1024 + k * 128: eA * 1024 + (k + 1) * 128,
                                   fh * 1024:(fh + 1) * 1024],
                                wg[eB * 1024 + k * 128: eB * 1024 + (k + 1) * 128,
                                   fh * 1024:(fh + 1) * 1024]))
                    for fh in range(2):
                        for k in range(8):
                            slot_srcs.append((
                                wu[eA * 1024 + k * 128: eA * 1024 + (k + 1) * 128,
                                   fh * 1024:(fh + 1) * 1024],
                                wu[eB * 1024 + k * 128: eB * 1024 + (k + 1) * 128,
                                   fh * 1024:(fh + 1) * 1024]))
                    for fc in range(16):
                        slot_srcs.append((
                            wd[eA * 2048 + fc * 128: eA * 2048 + (fc + 1) * 128, :],
                            wd[eB * 2048 + fc * 128: eB * 2048 + (fc + 1) * 128, :]))

                def emit_slot_dma(j):
                    r = j % RING
                    if ring_rel[r] is not None:
                        pl.wait("gpsimd", ring_rel[r])
                    a_src, b_src = slot_srcs[j]
                    sl = ring[:, r * 2048: r * 2048 + 1024]
                    sr = ring[:, r * 2048 + 1024: (r + 1) * 2048]
                    pl.dma("gpsimd", sl, a_src, "qg")
                    pl.dma("gpsimd", sr, b_src, "qg")
                    qg_slot_end.append(pl.snap("qg"))

                def slot_ap(j):
                    r = j % RING
                    return ring[:, r * 2048:(r + 1) * 2048]

                for j in range(min(RING, len(slot_srcs))):
                    emit_slot_dma(j)
                next_dma = [RING]

                def consume_slot(j, passes_fn):
                    pl.wait("tensor", qg_slot_end[j])
                    passes_fn(slot_ap(j))
                    ring_rel[j % RING] = pl.snap("pe")
                    if next_dma[0] < len(slot_srcs):
                        emit_slot_dma(next_dma[0])
                        next_dma[0] += 1

                for m in range(8):
                    with nc.psum_tensor("lps%d" % m + u, [128, 4096], F32) as lps:
                        pl.wait("tensor", ln_done)
                        for d in range(8):
                            for nb in range(8):
                                pl.op("tensor", "matmul",
                                      lps[:, nb * 512:(nb + 1) * 512],
                                      seT[:, d * 1024 + m * 128:
                                          d * 1024 + (m + 1) * 128],
                                      xnT[:, d * 4096 + nb * 512:
                                          d * 4096 + (nb + 1) * 512],
                                      start=(d == 0), stop=(d == 7),
                                      inc="pe" if (d == 7 and nb == 7) else None)
                        pl.wait("vector", pl.snap("pe"))
                        pl.op("vector", "reduce_max", stt[:, 8:9], lps[:],
                              axis=AX.X, negate=True, inc="dve")
                        negmx = pl.snap("dve")
                        lps_bf = lps[:].bitcast(BF16)
                        stg_rel = [None, None]
                        for j in range(8):
                            hh = j % 2
                            pl.wait("scalar", negmx)
                            if stg_rel[hh] is not None:
                                pl.wait("scalar", stg_rel[hh])
                            pl.op("scalar", "activation",
                                  stg[:, hh * 512:(hh + 1) * 512],
                                  lps[:, j * 512:(j + 1) * 512], AF.Exp,
                                  bias=stt[:, 8:9], accum_out=ssc[:, j:j + 1],
                                  inc="act")
                            pl.wait("tensor", pl.snap("act"))
                            for q in range(4):
                                pl.op("tensor", "transpose",
                                      lps_bf[:, j * 1024 + q * 128:
                                             j * 1024 + (q + 1) * 128],
                                      stg[:, hh * 512 + q * 128:
                                          hh * 512 + (q + 1) * 128],
                                      idb_s[:], inc="pe" if q == 3 else None)
                            stg_rel[hh] = pl.snap("pe")
                            pl.wait("vector", pl.snap("pe"))
                            pl.op("vector", "tensor_copy",
                                  dT[:, j * 512:(j + 1) * 512],
                                  lps_bf[:, j * 1024: j * 1024 + 512], inc="dve")
                        pl.opf("vector", "reduce_sum", stt[:, 9:10], ssc[:, 0:8],
                               axis=AX.X)
                        pl.opf("vector", "reciprocal", rcpS[:, m:m + 1],
                               stt[:, 9:10])
                        pl.op("vector", "engine_nop", inc="dve")
                        dT_done = pl.snap("dve")
                        if dump and u == "A" and m == 0:
                            pl.wait("sync", dT_done)
                            pl.dma("sync", dbg["d_dT"], dT[:], "qs")
                            pl.dma("sync", dbg["d_stt"], stt[:], "qs")
                            pl.dma("sync", dbg["d_ssc"], ssc[:], "qs")
                    with nc.psum_tensor("sps%d" % m + u, [128, 1024], F32) as sps, \
                         nc.psum_tensor("stp%d" % m + u, [128, 1024], BF16) as stp:
                        pl.wait("tensor", dT_done)
                        for k in range(NT):
                            for nh in range(2):
                                pl.op("tensor", "matmul",
                                      sps[:, nh * 512:(nh + 1) * 512],
                                      dT[:, k * 128:(k + 1) * 128],
                                      xn[:, k * 1024 + nh * 512:
                                         k * 1024 + (nh + 1) * 512],
                                      start=(k == 0), stop=(k == NT - 1),
                                      inc="pe" if (k == NT - 1 and nh == 1) else None)
                        pl.wait("vector", pl.snap("pe"))
                        pl.op("vector", "tensor_scalar", slot_bf[:], sps[:],
                              rcpS[:, m:m + 1], None, ALU.mult, inc="dve")
                        pl.wait("tensor", pl.snap("dve"))
                        for d in range(8):
                            pl.op("tensor", "transpose",
                                  stp[:, d * 128:(d + 1) * 128],
                                  slot_bf[:, d * 128:(d + 1) * 128], idb_s[:],
                                  inc="pe" if d == 7 else None)
                        pl.wait("vector", pl.snap("pe"))
                        pl.op("vector", "tensor_copy",
                              _r3(hT, 1024)[:, :, m * 128:(m + 1) * 128],
                              _r3(stp, 128), inc="dve")
                        ln_done = pl.snap("dve")
                        if dump and u == "A" and m == 0:
                            pl.wait("sync", pl.snap("dve"))
                            pl.dma("sync", dbg["d_slot"], slot_bf[:], "qs")
                        if dump and u == "A" and m == 7:
                            pl.wait("sync", pl.snap("dve"))
                            pl.dma("sync", dbg["d_hT"], hT[:], "qs")

                pl.barrier()
                AS2.close()

                eout = FE(nc.sbuf_tensor("eout" + u, [128, ND * 1024], BF16))
                with nc.sbuf_tensor("gsb" + u, [128, 2048], BF16) as gsb, \
                     nc.sbuf_tensor("usb" + u, [128, 2048], BF16) as usb, \
                     nc.sbuf_tensor("gub" + u, [128, 2048], BF16) as gub, \
                     nc.sbuf_tensor("guT" + u, [128, 2048], BF16) as guT, \
                     nc.sbuf_tensor("bstgA" + u, [1, 5120], BF16) as bstgA, \
                     nc.sbuf_tensor("bstgB" + u, [1, 5120], BF16) as bstgB, \
                     nc.psum_tensor("gpA" + u, [128, 1024], F32) as gpA, \
                     nc.psum_tensor("gpB" + u, [128, 1024], F32) as gpB, \
                     nc.psum_tensor("gtp" + u, [128, 2048], BF16) as gtp:
                    jbase = 0
                    eout_rel = None
                    usb_rel = None
                    bias_rel = None
                    npairs = 0 if "noexp" in skip else (1 if "exp0" in skip else 8)
                    for p in range(npairs):
                        if "biasdma" in skip:
                            bias_ld = pl.snap("qg")
                        if bias_rel is not None and "biasdma" not in skip:
                            pl.wait("gpsimd", bias_rel)
                        if "biasdma" not in skip:
                            pl.dma("gpsimd", bstgA[0:1, 0:2048],
                                   bgE[0:1, p * 2048:(p + 1) * 2048], "qg")
                        if "biasdma" not in skip:
                            pl.dma("gpsimd", bstgB[0:1, 0:2048],
                                   bgE[1:2, p * 2048:(p + 1) * 2048], "qg")
                            pl.dma("gpsimd", bstgA[0:1, 2048:4096],
                                   buE[0:1, p * 2048:(p + 1) * 2048], "qg")
                            pl.dma("gpsimd", bstgB[0:1, 2048:4096],
                                   buE[1:2, p * 2048:(p + 1) * 2048], "qg")
                            pl.dma("gpsimd", bstgA[0:1, 4096:5120],
                                   bdE[0:1, p * 1024:(p + 1) * 1024], "qg")
                            pl.dma("gpsimd", bstgB[0:1, 4096:5120],
                                   bdE[1:2, p * 1024:(p + 1) * 1024], "qg")
                            bias_ld = pl.snap("qg")
                        mats = (("g", 0),) if ("gateonly" in skip) else \
                            (("g", 0), ("u", 2048))
                        for mat, boff in mats:
                            if mat == "g" and eout_rel is not None:
                                pl.wait("tensor", eout_rel)
                            if mat == "u":
                                pl.wait("tensor", gdone)
                            for fh in ((0,) if "fh0" in skip else (0, 1)):
                                if fh == 1:
                                    pl.wait("tensor",
                                            gdone if mat == "g" else usb_rel)
                                for k in range(8):
                                    j = jbase + fh * 8 + k

                                    def passes(sl, p=p, k=k):
                                        lh = hT[:, k * 1024 + p * 128:
                                                k * 1024 + (p + 1) * 128]
                                        for ab, pc in ((0, gpA), (1, gpB)):
                                            for nh in range(2):
                                                pl.op(
                                                    "tensor", "matmul",
                                                    pc[:, nh * 512:(nh + 1) * 512],
                                                    lh,
                                                    sl[:, ab * 1024 + nh * 512:
                                                       ab * 1024 + (nh + 1) * 512],
                                                    start=(k == 0), stop=False,
                                                    inc="pe" if (ab == 1 and
                                                                 nh == 1) else None)
                                    consume_slot(j, passes)
                                if "bias" not in skip:
                                    pl.wait("tensor", bias_ld)
                                    for ab, pc, bs in ((0, gpA, bstgA), (1, gpB, bstgB)):
                                        for nh in range(2):
                                            pl.op("tensor", "matmul",
                                                  pc[:, nh * 512:(nh + 1) * 512],
                                                  one1_s[0:1, :],
                                                  bs[0:1, boff + fh * 1024 + nh * 512:
                                                     boff + fh * 1024 + (nh + 1) * 512],
                                                  start=False, stop=True,
                                                  inc="pe" if (ab == 1 and
                                                               nh == 1) else None)
                                if mat == "g":
                                    if "nogelu" in skip:
                                        pl.op("scalar", "nop", inc="act")
                                        gdone = pl.snap("act")
                                    else:
                                        pl.wait("scalar", pl.snap("pe"))
                                        pl.op("scalar", "activation",
                                              gsb[0:64, fh * 1024:(fh + 1) * 1024],
                                              gpA[0:64, :], AF.Gelu)
                                        pl.op("scalar", "activation",
                                              gsb[64:128, fh * 1024:(fh + 1) * 1024],
                                              gpB[64:128, :], AF.Gelu, inc="act")
                                        gdone = pl.snap("act")
                                    if "gelubar" in skip:
                                        pl.barrier()
                                else:
                                    pl.wait("vector", pl.snap("pe"))
                                    pl.op("vector", "tensor_copy",
                                          usb[0:64, fh * 1024:(fh + 1) * 1024],
                                          gpA[0:64, :])
                                    pl.opf("vector", "tensor_copy",
                                           usb[64:128, fh * 1024:(fh + 1) * 1024],
                                           gpB[64:128, :])
                                    usb_rel = pl.snap("dve")
                            jbase += 16
                        if "gateonly" in skip:
                            jbase += 32
                            continue
                        pl.wait("vector", gdone)
                        pl.op("vector", "tensor_tensor", gub[:], gsb[:],
                              usb[:], ALU.mult, inc="dve")
                        pl.wait("tensor", pl.snap("dve"))
                        pl.wait("tensor", usb_rel)
                        for fc in range(16):
                            pl.op("tensor", "transpose",
                                  gtp[:, fc * 128:(fc + 1) * 128],
                                  gub[:, fc * 128:(fc + 1) * 128], idb_s[:],
                                  inc="pe" if fc == 15 else None)
                        pl.wait("vector", pl.snap("pe"))
                        pl.op("vector", "tensor_copy", guT[:], gtp[:], inc="dve")
                        pl.wait("tensor", pl.snap("dve"))
                        if eout_rel is not None:
                            pl.wait("tensor", eout_rel)
                        if "nodown" in skip:
                            jbase += 16
                            continue
                        for fc in range(16):
                            j = jbase + fc

                            def dpasses(sl, fc=fc):
                                lh = guT[:, fc * 128:(fc + 1) * 128]
                                for ab, pc in ((0, gpA), (1, gpB)):
                                    for nh in range(2):
                                        pl.op("tensor", "matmul",
                                              pc[:, nh * 512:(nh + 1) * 512],
                                              lh,
                                              sl[:, ab * 1024 + nh * 512:
                                                 ab * 1024 + (nh + 1) * 512],
                                              start=(fc == 0), stop=False,
                                              inc="pe" if (ab == 1 and
                                                           nh == 1) else None)
                            consume_slot(j, dpasses)
                        jbase += 16
                        if "bias" not in skip:
                            for ab, pc, bs in ((0, gpA, bstgA), (1, gpB, bstgB)):
                                for nh in range(2):
                                    pl.op("tensor", "matmul",
                                          pc[:, nh * 512:(nh + 1) * 512],
                                          one1_s[0:1, :],
                                          bs[0:1, 4096 + nh * 512:
                                             4096 + (nh + 1) * 512],
                                          start=False, stop=True,
                                          inc="pe" if (ab == 1 and nh == 1) else None)
                        bias_rel = pl.snap("pe")
                        pl.wait("vector", pl.snap("pe"))
                        pl.op("vector", "tensor_copy",
                              eout[0:64, p * 1024:(p + 1) * 1024], gpA[0:64, :])
                        pl.op("vector", "tensor_copy",
                              eout[64:128, p * 1024:(p + 1) * 1024],
                              gpB[64:128, :], inc="dve")
                        eout_rel = pl.snap("dve")
                if dump and u == "A" and eout_rel is not None:
                    pl.wait("sync", eout_rel)
                    pl.dma("sync", dbg["d_eout"], eout[:], "qs")
                with nc.sbuf_tensor("cstg" + u, [128, 1024], BF16) as cstg, \
                     nc.sbuf_tensor("cTs" + u, [128, 1024], BF16) as cTs, \
                     nc.sbuf_tensor("yrot" + u, [128, 2 * 1024], F32) as yrot, \
                     nc.psum_tensor("cps" + u, [128, 2 * 1024], F32) as cps, \
                     nc.psum_tensor("ctp" + u, [128, 1024], BF16) as ctp, \
                     nc.psum_tensor("ops" + u, [128, 1024], F32) as ops:
                    yst_rel = [None, None]
                    cstg_rel = None
                    cT_rel = None
                    ops_rel = None
                    exp_rel = [None, None]
                    if eout_rel is not None:
                        pl.wait("tensor", eout_rel)

                    def emit_logits2(i):
                        h = i % 2
                        cph = cps[:, h * 1024:(h + 1) * 1024]
                        if exp_rel[h] is not None:
                            pl.wait("tensor", exp_rel[h])
                        for d in range(8):
                            for nh in range(2):
                                pl.op("tensor", "matmul",
                                      cph[:, nh * 512:(nh + 1) * 512],
                                      xnT[:, d * 4096 + i * 128:
                                          d * 4096 + (i + 1) * 128],
                                      seT[:, d * 1024 + nh * 512:
                                          d * 1024 + (nh + 1) * 512],
                                      start=(d == 0), stop=(d == 7),
                                      inc="pe" if (d == 7 and nh == 1) else None)
                        return pl.snap("pe")

                    log_done = [None] * NT
                    log_done[0] = emit_logits2(0)
                    for i in range(NT):
                        h = i % 2
                        if xrot_rel[h] is not None:
                            pl.wait("scalar", xrot_rel[h])
                        pl.dma("scalar", xrot[:, h * 1024:(h + 1) * 1024],
                               src[i * 128:(i + 1) * 128, :], "qc")
                        xls = pl.snap("qc")
                        cph = cps[:, h * 1024:(h + 1) * 1024]
                        if i + 1 < NT:
                            log_done[i + 1] = emit_logits2(i + 1)
                        pl.wait("vector", log_done[i])
                        pl.op("vector", "reduce_max", stt[:, 10:11], cph,
                              axis=AX.X, negate=True, inc="dve")
                        pl.wait("scalar", pl.snap("dve"))
                        if cstg_rel is not None:
                            pl.wait("scalar", cstg_rel)
                        pl.op("scalar", "activation", cstg[:], cph, AF.Exp,
                              bias=stt[:, 10:11], accum_out=stt[:, 11:12],
                              inc="act")
                        exp_rel[h] = pl.snap("act")
                        pl.wait("vector", pl.snap("act"))
                        pl.opf("vector", "reciprocal", stt[:, 12:13], stt[:, 11:12])
                        pl.opf("vector", "tensor_scalar", stt[:, 13:14],
                               stt[:, 12:13], 0.5, None, ALU.mult)
                        pl.wait("tensor", pl.snap("act"))
                        if cT_rel is not None:
                            pl.wait("tensor", cT_rel)
                        for q in range(8):
                            pl.op("tensor", "transpose",
                                  ctp[:, q * 128:(q + 1) * 128],
                                  cstg[:, q * 128:(q + 1) * 128], idb_s[:],
                                  inc="pe" if q == 7 else None)
                        cstg_rel = pl.snap("pe")
                        pl.wait("vector", pl.snap("pe"))
                        pl.op("vector", "tensor_copy", cTs[:], ctp[:], inc="dve")
                        pl.wait("tensor", pl.snap("dve"))
                        if ops_rel is not None:
                            pl.wait("tensor", ops_rel)
                        for k in range(8):
                            for nh in range(2):
                                pl.op("tensor", "matmul",
                                      ops[:, nh * 512:(nh + 1) * 512],
                                      cTs[:, k * 128:(k + 1) * 128],
                                      eout[:, k * 1024 + nh * 512:
                                           k * 1024 + (nh + 1) * 512],
                                      start=(k == 0), stop=(k == 7),
                                      inc="pe" if (k == 7 and nh == 1) else None)
                        cT_rel = pl.snap("pe")
                        pl.wait("vector", pl.snap("pe"))
                        pl.wait("vector", xls)
                        if yst_rel[h] is not None:
                            pl.wait("vector", yst_rel[h])
                        pl.op("vector", "scalar_tensor_tensor",
                              yrot[:, h * 1024:(h + 1) * 1024], ops[:],
                              stt[:, 13:14], xrot[:, h * 1024:(h + 1) * 1024],
                              ALU.mult, ALU.add, inc="dve")
                        xrot_rel[h] = pl.snap("dve")
                        ops_rel = pl.snap("dve")
                        pl.wait("scalar", pl.snap("dve"))
                        pl.dma("scalar", dst[i * 128:(i + 1) * 128, :],
                               yrot[:, h * 1024:(h + 1) * 1024], "qc")
                        yst_rel[h] = pl.snap("qc")
            pl.barrier()

        def attn(src, dst):
            with ExitStack() as AS:
                AE = AS.enter_context
                wbf = AE(nc.sbuf_tensor("wbf", [128, 4 * 8192], BF16))
                lngb = AE(nc.sbuf_tensor("lngb", [128, 1024], BF16))
                lnbb = AE(nc.sbuf_tensor("lnbb", [128, 1024], BF16))
                xch = AE(nc.sbuf_tensor("xch", [128, 4 * 1024], F32))
                xa_stg = AE(nc.sbuf_tensor("xa_stg", [128, 1024], BF16))
                xaT = AE(nc.sbuf_tensor("xaT", [128, 4096], BF16))
                lastT = AE(nc.sbuf_tensor("lastT", [128, 512], BF16))
                dbuf = AE(nc.sbuf_tensor("dbuf", [128, 512], BF16))
                kinT = AE(nc.sbuf_tensor("kinT", [128, 4096], BF16))
                vinT = AE(nc.sbuf_tensor("vinT", [128, 4096], BF16))
                rinT = AE(nc.sbuf_tensor("rinT", [128, 4096], BF16))
                ekT = AE(nc.sbuf_tensor("ekT", [128, 4096], BF16))
                vT = AE(nc.sbuf_tensor("vT", [128, 4096], BF16))
                emT = AE(nc.sbuf_tensor("emT", [128, 4096], BF16))
                ubuf = AE(nc.sbuf_tensor("ubuf", [128, 512], F32))
                wbuf2 = AE(nc.sbuf_tensor("wbuf2", [128, 512], F32))
                rcb = AE(nc.sbuf_tensor("rcb", [128, 512], F32))
                numT = AE(nc.sbuf_tensor("numT", [128, 4096], F32))
                denT = AE(nc.sbuf_tensor("denT", [128, 4096], BF16))
                gatedT = AE(nc.sbuf_tensor("gatedT", [128, 4096], BF16))
                carry = AE(nc.sbuf_tensor("carry", [128, 8], BF16))
                snum = AE(nc.sbuf_tensor("snum", [128, 8], F32))
                sden = AE(nc.sbuf_tensor("sden", [128, 8], F32))
                stt2 = AE(nc.sbuf_tensor("stt2", [128, 16], F32))

                with nc.psum_tensor("bc2", [128, 1024], F32) as bc2:
                    bcast_vec(lng, lngb, bc2, "lng")
                    bcast_vec(lnb, lnbb, bc2, "lnb")
                for w_i, wsrc in enumerate((wkT, wvT, wrT, woT)):
                    for ck in range(8):
                        pl.dma("gpsimd",
                               wbf[:, w_i * 8192 + ck * 1024:
                                   w_i * 8192 + (ck + 1) * 1024],
                               wsrc[ck * 128:(ck + 1) * 128, :], "qg")
                wld = pl.snap("qg")
                pl.op("vector", "memset", carry[:], 0.0)
                pl.op("vector", "memset", snum[:], 0.0)
                pl.op("vector", "memset", sden[:], 0.0, inc="dve")
                init_done = pl.snap("dve")
                pl.wait("tensor", wld)
                pl.wait("tensor", init_done)

                xch_rel = None
                yst_rel = [None, None]
                for ch in range(NCH):
                    if xch_rel is not None:
                        pl.wait("scalar", xch_rel)
                    for tsub in range(4):
                        pl.dma("scalar", xch[:, tsub * 1024:(tsub + 1) * 1024],
                               src[(ch * 4 + tsub) * 128:
                                   (ch * 4 + tsub + 1) * 128, :], "qc")
                    xls = pl.snap("qc")
                    pl.wait("vector", xls)
                    pl.wait("scalar", xls)
                    if xch_rel is not None:
                        pl.wait("tensor", xch_rel)
                    with nc.psum_tensor("atp%d" % ch, [128, 2 * 1024], BF16) as atp, \
                         nc.psum_tensor("sqa%d" % ch, [128, 1024], F32) as sqa:
                        for tsub in range(4):
                            xi = xch[:, tsub * 1024:(tsub + 1) * 1024]
                            hh = tsub % 2
                            pl.opf("vector", "reduce_sum", stt2[:, 0:1], xi,
                                   axis=AX.X)
                            s1d = pl.snap("dve")
                            pl.op("scalar", "activation", sqa[:, 0:1024], xi,
                                  AF.Square, accum_out=stt2[:, 1:2])
                            pl.wait("scalar", s1d)
                            pl.op("scalar", "activation", stt2[:, 4:5],
                                  stt2[:, 0:1], AF.Square,
                                  scale=(1.0 + EPS) ** 0.5 / D)
                            pl.op("scalar", "activation", stt2[:, 5:6],
                                  stt2[:, 1:2], AF.Copy,
                                  scale=(1.0 + EPS) / D, bias=EPS * EPS,
                                  inc="act")
                            pl.wait("vector", pl.snap("act"))
                            pl.opf("vector", "tensor_tensor", stt2[:, 5:6],
                                   stt2[:, 5:6], stt2[:, 4:5], ALU.subtract)
                            pl.wait("scalar", pl.snap("dve"))
                            pl.op("scalar", "activation", stt2[:, 6:7],
                                  stt2[:, 5:6], AF.Ln)
                            pl.op("scalar", "activation", stt2[:, 6:7],
                                  stt2[:, 6:7], AF.Exp, scale=-0.5, inc="act")
                            pl.wait("vector", pl.snap("act"))
                            pl.opf("vector", "scalar_tensor_tensor", stt2[:, 7:8],
                                   stt2[:, 0:1], 1.0 / D, stt2[:, 6:7],
                                   ALU.mult, ALU.mult)
                            pl.opf("vector", "tensor_scalar", xa_stg[:], xi,
                                   stt2[:, 6:7], stt2[:, 7:8],
                                   ALU.mult, ALU.subtract)
                            pl.opf("vector", "tensor_tensor", xa_stg[:],
                                   xa_stg[:], lngb[:], ALU.mult)
                            pl.op("vector", "tensor_tensor", xa_stg[:],
                                  xa_stg[:], lnbb[:], ALU.add, inc="dve")
                            pl.wait("tensor", pl.snap("dve"))
                            tph = atp[:, hh * 1024:(hh + 1) * 1024]
                            for c in range(8):
                                pl.op("tensor", "transpose",
                                      tph[:, c * 128:(c + 1) * 128],
                                      xa_stg[:, c * 128:(c + 1) * 128],
                                      idb_s[:], inc="pe" if c == 7 else None)
                            pl.wait("vector", pl.snap("pe"))
                            pl.op("vector", "tensor_copy",
                                  _r3(xaT, 512)[:, :, tsub * 128:(tsub + 1) * 128],
                                  _r3(tph, 128), inc="dve")
                    for c in range(8):
                        xc_ap = xaT[:, c * 512:(c + 1) * 512]
                        pl.op("vector", "tensor_copy", lastT[:, 0:1],
                              carry[:, c:c + 1])
                        pl.opf("vector", "tensor_copy", lastT[:, 1:512],
                               xaT[:, c * 512: c * 512 + 511])
                        pl.op("vector", "tensor_copy", carry[:, c:c + 1],
                              xaT[:, c * 512 + 511: c * 512 + 512])
                        pl.opf("vector", "tensor_tensor", dbuf[:], xc_ap,
                               lastT[:], ALU.subtract)
                        pl.op("vector", "scalar_tensor_tensor",
                              kinT[:, c * 512:(c + 1) * 512], dbuf[:],
                              tmk_s[:, c:c + 1], lastT[:], ALU.mult, ALU.add)
                        pl.op("vector", "scalar_tensor_tensor",
                              vinT[:, c * 512:(c + 1) * 512], dbuf[:],
                              tmv_s[:, c:c + 1], lastT[:], ALU.mult, ALU.add)
                        pl.op("vector", "scalar_tensor_tensor",
                              rinT[:, c * 512:(c + 1) * 512], dbuf[:],
                              tmr_s[:, c:c + 1], lastT[:], ALU.mult, ALU.add,
                              inc="dve")
                    mix_done = pl.snap("dve")
                    with nc.psum_tensor("prj%d" % ch, [128, 4096], F32) as prj:
                        pl.wait("tensor", mix_done)
                        ev_snap = None

                        def proj(w_i, src_t, ev_fn):
                            nonlocal ev_snap
                            for mo in range(8):
                                if ev_snap is not None:
                                    pl.wait("tensor", ev_snap[mo])
                                for kc in range(8):
                                    pl.op("tensor", "matmul",
                                          prj[:, mo * 512:(mo + 1) * 512],
                                          wbf[:, w_i * 8192 + kc * 1024 + mo * 128:
                                              w_i * 8192 + kc * 1024 + (mo + 1) * 128],
                                          src_t[:, kc * 512:(kc + 1) * 512],
                                          start=(kc == 0), stop=(kc == 7),
                                          inc="pe" if kc == 7 else None)
                                ev_fn(mo, pl.snap("pe"))

                        ek_evs = []

                        def ek_ev(mo, s):
                            pl.wait("scalar", s)
                            pl.op("scalar", "activation",
                                  ekT[:, mo * 512:(mo + 1) * 512],
                                  prj[:, mo * 512:(mo + 1) * 512], AF.Exp,
                                  inc="act")
                            ek_evs.append(pl.snap("act"))
                        proj(0, kinT, ek_ev)
                        ev_snap = ek_evs

                        v_evs = []

                        def v_ev(mo, s):
                            pl.wait("vector", s)
                            pl.op("vector", "tensor_copy",
                                  vT[:, mo * 512:(mo + 1) * 512],
                                  prj[:, mo * 512:(mo + 1) * 512], inc="dve")
                            v_evs.append(pl.snap("dve"))
                        proj(1, vinT, v_ev)
                        ev_snap = v_evs

                        r_evs = []

                        def r_ev(mo, s):
                            pl.wait("scalar", s)
                            pl.op("scalar", "activation",
                                  emT[:, mo * 512:(mo + 1) * 512],
                                  prj[:, mo * 512:(mo + 1) * 512], AF.Exp,
                                  scale=-1.0, inc="act")
                            r_evs.append(pl.snap("act"))
                        proj(2, rinT, r_ev)
                        r_done = pl.snap("act")

                        for c in range(8):
                            pl.wait("vector", v_evs[c])
                            td_b = tdE[:, c:c + 1].broadcast_to([128, 512])
                            pl.opf("vector", "tensor_tensor", ubuf[:],
                                   ekT[:, c * 512:(c + 1) * 512],
                                   vT[:, c * 512:(c + 1) * 512], ALU.mult)
                            pl.op("vector", "tensor_tensor_scan",
                                  numT[:, c * 512:(c + 1) * 512], td_b,
                                  ubuf[:], snum[:, c:c + 1], ALU.mult, ALU.add)
                            pl.opf("vector", "tensor_tensor_scan",
                                   denT[:, c * 512:(c + 1) * 512], td_b,
                                   ekT[:, c * 512:(c + 1) * 512],
                                   sden[:, c:c + 1],
                                   ALU.mult, ALU.add)
                            pl.op("vector", "tensor_copy", snum[:, c:c + 1],
                                  numT[:, c * 512 + 511: c * 512 + 512])
                            pl.opf("vector", "tensor_copy", sden[:, c:c + 1],
                                   denT[:, c * 512 + 511: c * 512 + 512])
                            pl.opf("vector", "tensor_scalar",
                                   denT[:, c * 512:(c + 1) * 512],
                                   denT[:, c * 512:(c + 1) * 512],
                                   1e-8, None, ALU.add)
                            pl.opf("vector", "reciprocal", rcb[:],
                                   denT[:, c * 512:(c + 1) * 512])
                            pl.opf("vector", "tensor_tensor", wbuf2[:],
                                   numT[:, c * 512:(c + 1) * 512],
                                   rcb[:], ALU.mult)
                            if c == 0:
                                pl.wait("vector", r_done)
                                pl.opf("vector", "tensor_scalar", emT[:],
                                       emT[:], 1.0, None, ALU.add)
                            pl.opf("vector", "reciprocal", rcb[:],
                                   emT[:, c * 512:(c + 1) * 512])
                            pl.op("vector", "tensor_tensor",
                                  gatedT[:, c * 512:(c + 1) * 512], wbuf2[:],
                                  rcb[:], ALU.mult,
                                  inc="dve")
                        gated_done = pl.snap("dve")
                    with nc.psum_tensor("ops2%d" % ch, [128, 2 * 1024], F32) as ops2:
                        pl.wait("tensor", gated_done)
                        oph_rel = [None, None]
                        for tsub in range(4):
                            hh = tsub % 2
                            oph = ops2[:, hh * 1024:(hh + 1) * 1024]
                            if oph_rel[hh] is not None:
                                pl.wait("tensor", oph_rel[hh])
                            for c in range(8):
                                for nh in range(2):
                                    pl.op("tensor", "matmul",
                                          oph[:, nh * 512:(nh + 1) * 512],
                                          gatedT[:, c * 512 + tsub * 128:
                                                 c * 512 + (tsub + 1) * 128],
                                          wbf[:, 3 * 8192 + c * 1024 + nh * 512:
                                              3 * 8192 + c * 1024 + (nh + 1) * 512],
                                          start=(c == 0), stop=(c == 7),
                                          inc="pe" if (c == 7 and nh == 1) else None)
                            pl.wait("vector", pl.snap("pe"))
                            xt = xch[:, tsub * 1024:(tsub + 1) * 1024]
                            pl.op("vector", "tensor_tensor", xt, oph, xt,
                                  ALU.add, inc="dve")
                            oph_rel[hh] = pl.snap("dve")
                            pl.wait("scalar", pl.snap("dve"))
                            pl.dma("scalar",
                                   dst[(ch * 4 + tsub) * 128:
                                       (ch * 4 + tsub + 1) * 128, :],
                                   xt, "qc")
                        xch_rel = pl.snap("qc")
            pl.barrier()

        ffn(x_in, y if dbg_stage == 1 else xres, "A")
        if dbg_stage >= 2:
            attn(xres, y if dbg_stage == 2 else xres)
        if dbg_stage >= 3:
            ffn(xres, y, "B")

        pl.wait("scalar", pl.snap("qc"))
        pl.wait("gpsimd", pl.snap("qg"))

        with nc.Block() as block:
            @block.sync
            def _(eng):
                for f in pl.streams["sync"]:
                    f(eng)

            @block.tensor
            def _(eng):
                for f in pl.streams["tensor"]:
                    f(eng)

            @block.vector
            def _(eng):
                for f in pl.streams["vector"]:
                    f(eng)

            @block.scalar
            def _(eng):
                for f in pl.streams["scalar"]:
                    f(eng)

            @block.gpsimd
            def _(eng):
                for f in pl.streams["gpsimd"]:
                    f(eng)

    return nc


_NC_CACHE = {}


def _get_nc(dbg_stage=3, dump=False, paranoid=False, skip=()):
    key = (dbg_stage, dump, paranoid, tuple(skip))
    if key not in _NC_CACHE:
        _NC_CACHE[key] = build(dbg_stage, dump, paranoid, skip)
    return _NC_CACHE[key]


def kernel(L0_attn_x, L0_attn_time_decay, L0_attn_time_mix_k, L0_attn_time_mix_v,
           L0_attn_time_mix_r, L0_attn_Wk, L0_attn_Wv, L0_attn_Wr, L0_attn_Wo,
           L0_attn_ln_g, L0_attn_ln_b, L0_ffn_norm_g, L0_ffn_norm_b, L0_ffn_slot_g,
           L0_ffn_slot_b, L0_ffn_scale, L0_ffn_slot_embeds, L0_ffn_Wg, L0_ffn_bg,
           L0_ffn_Wu, L0_ffn_bu, L0_ffn_Wd, L0_ffn_bd, dbg_stage=3, dump=False,
           paranoid=False, skip=()):
    f32 = np.float32
    C = np.ascontiguousarray

    def vecP(v):
        return C(np.asarray(v, f32).reshape(8, 128).T)

    def rowE(b, w):
        a = np.asarray(b, f32).reshape(8, 2, w).transpose(1, 0, 2).reshape(2, 8 * w)
        return C(a)

    common = {
        "wkT": C(np.asarray(L0_attn_Wk, f32).T),
        "wvT": C(np.asarray(L0_attn_Wv, f32).T),
        "wrT": C(np.asarray(L0_attn_Wr, f32).T),
        "woT": C(np.asarray(L0_attn_Wo, f32).T),
        "wg": C(np.asarray(L0_ffn_Wg, f32).reshape(NE * D, FF)),
        "wu": C(np.asarray(L0_ffn_Wu, f32).reshape(NE * D, FF)),
        "wd": C(np.asarray(L0_ffn_Wd, f32).reshape(NE * FF, D)),
        "semb": C(np.asarray(L0_ffn_slot_embeds, f32).reshape(1024, D)),
        "bgE": rowE(L0_ffn_bg, FF),
        "buE": rowE(L0_ffn_bu, FF),
        "bdE": rowE(L0_ffn_bd, D),
        "tdP": vecP(L0_attn_time_decay),
        "tmkP": vecP(L0_attn_time_mix_k),
        "tmvP": vecP(L0_attn_time_mix_v),
        "tmrP": vecP(L0_attn_time_mix_r),
        "lng": C(np.asarray(L0_attn_ln_g, f32).reshape(1, D)),
        "lnb": C(np.asarray(L0_attn_ln_b, f32).reshape(1, D)),
        "nog": C(np.asarray(L0_ffn_norm_g, f32).reshape(1, D)),
        "nob": C(np.asarray(L0_ffn_norm_b, f32).reshape(1, D)),
        "slg": C(np.asarray(L0_ffn_slot_g, f32).reshape(1, D)),
        "slb": C(np.asarray(L0_ffn_slot_b, f32).reshape(1, D)),
        "scl": C(np.asarray(L0_ffn_scale, f32).reshape(1, 1)),
        "idb": np.eye(128, dtype=bf),
        "e2": np.concatenate([
            np.concatenate([np.ones((1, 64)), np.zeros((1, 64))], 1),
            np.concatenate([np.zeros((1, 64)), np.ones((1, 64))], 1)],
            0).astype(bf),
        "one1": np.ones((1, 128), dtype=bf),
    }
    xs = np.asarray(L0_attn_x, f32)
    in_maps = [dict(common, x=C(xs[c])) for c in range(8)]
    nc = _get_nc(dbg_stage, dump, paranoid, skip)
    res = run_bass_kernel_spmd(nc, in_maps, core_ids=list(range(8)))
    out = np.stack([res.results[c]["y"] for c in range(8)], axis=0)
    if dump:
        return out.astype(np.float32), res.results
    return out.astype(np.float32)


# version 03
# speedup vs baseline: 1.0541x; 1.0263x over previous
import numpy as np
import ml_dtypes
from contextlib import ExitStack

import concourse.bass as bass
import concourse.mybir as mybir
from concourse.bass_utils import run_bass_kernel_spmd

F32 = mybir.dt.float32
BF16 = mybir.dt.bfloat16
AF = mybir.ActivationFunctionType
ALU = mybir.AluOpType
AX = mybir.AxisListType
bf = ml_dtypes.bfloat16

T, D = 4096, 1024
NT = 32
ND = 8
NE, SS, FF = 16, 64, 2048
EPS = 1e-5
RING = 3
CHUNK = 512
NCH = T // CHUNK

ENGS = ("sync", "tensor", "vector", "scalar", "gpsimd")


class Plan:
    def __init__(self, nc, stack):
        self.nc = nc
        self.streams = {e: [] for e in ENGS}
        self.sems = {}
        for name in ("qs", "qc", "qg", "pe", "dve", "act", "bar"):
            h = stack.enter_context(nc.semaphore(name))
            self.sems[name] = [h, 0]

    def emit(self, eng, fn):
        self.streams[eng].append(fn)

    def op(self, eng, meth, *args, inc=None, incn=1, **kw):
        if inc is not None:
            s = self.sems[inc]
            s[1] += incn
            h = s[0]
        else:
            h = None

        def f(e, m=meth, a=args, k=kw, h=h, n=incn):
            r = getattr(e, m)(*a, **k)
            if h is not None:
                r.then_inc(h, n)
        self.emit(eng, f)

    def snap(self, sem):
        return (sem, self.sems[sem][1])

    def wait_lazy(self, eng):
        cell = [None]

        def f(e, cell=cell):
            if cell[0] is None:
                return
            h, v = cell[0]
            if v > 0:
                e.wait_ge(h, v)
        self.emit(eng, f)
        return cell

    def fill_lazy(self, cell, sem):
        s = self.sems[sem]
        cell[0] = (s[0], s[1])

    def wait(self, eng, snapshot):
        sem, v = snapshot
        if v <= 0:
            return
        h = self.sems[sem][0]
        self.emit(eng, lambda e, h=h, v=v: e.wait_ge(h, v))

    ENG_SEM = {"tensor": "pe", "vector": "dve", "scalar": "act"}

    def opf(self, eng, meth, *args, **kw):
        sem = self.ENG_SEM[eng]
        self.op(eng, meth, *args, inc=sem, **kw)
        self.wait(eng, self.snap(sem))

    def dma(self, eng, out, in_, q):
        self.op(eng, "dma_start", inc=q, incn=16, out=out, in_=in_)

    def barrier(self):
        s = self.sems["bar"]
        for eng in ENGS:
            self.emit(eng, lambda e, h=s[0]: e.sem_inc(h, 1))
        s[1] += len(ENGS)
        v = s[1]
        for eng in ENGS:
            self.emit(eng, lambda e, h=s[0], v=v: e.wait_ge(h, v))


def _r3(ap, c):
    return ap[:].rearrange("p (a b) -> p a b", b=c)


def build(dbg_stage=3, dump=False, paranoid=False, skip=()):
    nc = bass.Bass("TRN2", target_bir_lowering=False)
    dt_in = {}

    def din(name, shape, dtype=F32):
        t = nc.dram_tensor(name, shape, dtype, kind="ExternalInput")
        dt_in[name] = t.ap()
        return dt_in[name]

    x_in = din("x", [T, D])
    wkT = din("wkT", [D, D]); wvT = din("wvT", [D, D])
    wrT = din("wrT", [D, D]); woT = din("woT", [D, D])
    wg = din("wg", [NE * D, FF]); wu = din("wu", [NE * D, FF])
    wd = din("wd", [NE * FF, D])
    semb = din("semb", [1024, D])
    bgE = din("bgE", [2, 8 * FF]); buE = din("buE", [2, 8 * FF])
    bdE = din("bdE", [2, 8 * D])
    tdP = din("tdP", [128, 8]); tmkP = din("tmkP", [128, 8])
    tmvP = din("tmvP", [128, 8]); tmrP = din("tmrP", [128, 8])
    lng = din("lng", [1, D]); lnb = din("lnb", [1, D])
    nog = din("nog", [1, D]); nob = din("nob", [1, D])
    slg = din("slg", [1, D]); slb = din("slb", [1, D])
    scl = din("scl", [1, 1])
    idb = din("idb", [128, 128], BF16)
    e2d = din("e2", [2, 128], BF16)
    one1 = din("one1", [1, 128], BF16)

    xres = nc.dram_tensor("xres", [T, D], F32).ap()
    y = nc.dram_tensor("y", [T, D], F32, kind="ExternalOutput").ap()
    dbg = {}
    if dump:
        for nm, sh, dt in (("d_xn", [128, NT * 1024], BF16),
                           ("d_xnT", [128, ND * 4096], BF16),
                           ("d_dT", [128, 4096], BF16),
                           ("d_stt", [128, 16], F32),
                           ("d_ssc", [128, 8], F32),
                           ("d_slot", [128, 1024], BF16),
                           ("d_hT", [128, ND * 1024], BF16),
                           ("d_eout", [128, ND * 1024], BF16),
                           ("d_gate", [128, 2048], BF16),
                           ("d_up", [128, 2048], BF16),
                           ("d_gu", [128, 2048], BF16),
                           ("d_guT", [128, 2048], BF16),
                           ("d_seT", [128, ND * 1024], BF16)):
            dbg[nm] = nc.dram_tensor(nm, sh, dt, kind="ExternalOutput").ap()

    with ExitStack() as S:
        E = S.enter_context
        pl = Plan(nc, S)

        idb_s = E(nc.sbuf_tensor("idb_s", [128, 128], BF16))
        e2_s = E(nc.sbuf_tensor("e2_s", [2, 128], BF16))
        one1_s = E(nc.sbuf_tensor("one1_s", [1, 128], BF16))
        seT = E(nc.sbuf_tensor("seT", [128, ND * 1024], BF16))
        tdE = E(nc.sbuf_tensor("tdE", [128, 8], F32))
        tmk_s = E(nc.sbuf_tensor("tmk_s", [128, 8], F32))
        tmv_s = E(nc.sbuf_tensor("tmv_s", [128, 8], F32))
        tmr_s = E(nc.sbuf_tensor("tmr_s", [128, 8], F32))
        vrow = E(nc.sbuf_tensor("vrow", [1, D], F32))
        vrow_b = E(nc.sbuf_tensor("vrow_b", [1, D], BF16))
        sclb = E(nc.sbuf_tensor("sclb", [128, 1], F32))

        pl.dma("sync", idb_s[:], idb, "qs")
        pl.dma("sync", e2_s[:], e2d, "qs")
        pl.dma("sync", one1_s[:], one1, "qs")
        pl.dma("sync", tdE[:], tdP, "qs")
        pl.dma("sync", tmk_s[:], tmkP, "qs")
        pl.dma("sync", tmv_s[:], tmvP, "qs")
        pl.dma("sync", tmr_s[:], tmrP, "qs")
        ld0 = pl.snap("qs")
        pl.wait("scalar", ld0)
        pl.op("scalar", "activation", tdE[:], tdE[:], AF.Exp, inc="act")

        vrow_rel = [None]

        def bcast_vec(vec_dram, dst_bf, ps_pool, tag):
            if vrow_rel[0] is not None:
                pl.wait("sync", vrow_rel[0])
            pl.dma("sync", vrow[:], vec_dram, "qs")
            pl.wait("vector", pl.snap("qs"))
            pl.wait("vector", pl.snap("pe"))
            pl.op("vector", "tensor_copy", vrow_b[:], vrow[:], inc="dve")
            vrow_rel[0] = pl.snap("dve")
            pl.wait("tensor", pl.snap("dve"))
            pl.wait("tensor", ld0)
            for nb in range(2):
                pl.op("tensor", "matmul", ps_pool[:, nb * 512:(nb + 1) * 512],
                      one1_s[:], vrow_b[:, nb * 512:(nb + 1) * 512],
                      start=True, stop=True, inc="pe" if nb == 1 else None)
            pl.wait("vector", pl.snap("pe"))
            pl.op("vector", "tensor_copy", dst_bf[:], ps_pool[:], inc="dve")

        def bcast_scl(ps_pool):
            if vrow_rel[0] is not None:
                pl.wait("sync", vrow_rel[0])
            pl.dma("sync", vrow[:, 0:1], scl, "qs")
            pl.wait("vector", pl.snap("qs"))
            pl.wait("vector", pl.snap("pe"))
            pl.op("vector", "tensor_copy", vrow_b[:, 0:1], vrow[:, 0:1], inc="dve")
            vrow_rel[0] = pl.snap("dve")
            pl.wait("tensor", pl.snap("dve"))
            pl.op("tensor", "matmul", ps_pool[:, 0:1], one1_s[:], vrow_b[:, 0:1],
                  start=True, stop=True, inc="pe")
            pl.wait("vector", pl.snap("pe"))
            pl.op("vector", "tensor_copy", sclb[:], ps_pool[:, 0:1], inc="dve")

        with nc.sbuf_tensor("se_x", [128, 1024], F32) as se_x, \
             nc.sbuf_tensor("se_sq", [128, 1024], F32) as se_sq, \
             nc.sbuf_tensor("se_bf", [128, 1024], BF16) as se_bf, \
             nc.sbuf_tensor("se_t1", [128, 1024], BF16) as se_t1, \
             nc.sbuf_tensor("slgb", [128, 1024], BF16) as slgb, \
             nc.sbuf_tensor("slbb", [128, 1024], BF16) as slbb, \
             nc.sbuf_tensor("st8", [128, 8], F32) as st8, \
             nc.psum_tensor("pre_ps", [128, 1024], F32) as pre_ps, \
             nc.psum_tensor("pre_tp", [128, 1024], BF16) as pre_tp:
            bcast_vec(slg, slgb, pre_ps, "slg")
            bcast_vec(slb, slbb, pre_ps, "slb")
            bcast_scl(pre_ps)
            for st in range(8):
                pl.dma("sync", se_x[:], semb[st * 128:(st + 1) * 128, :], "qs")
                lsn = pl.snap("qs")
                pl.wait("vector", lsn)
                pl.wait("scalar", lsn)
                pl.opf("vector", "reduce_sum", st8[:, 0:1], se_x[:], axis=AX.X)
                s1_done = pl.snap("dve")
                pl.op("scalar", "activation", se_sq[:], se_x[:], AF.Square,
                      accum_out=st8[:, 1:2])
                pl.wait("scalar", s1_done)
                pl.op("scalar", "activation", st8[:, 4:5], st8[:, 0:1],
                      AF.Square, scale=1.0 / D)
                pl.op("scalar", "activation", st8[:, 5:6], st8[:, 1:2],
                      AF.Copy, scale=1.0 / D, bias=EPS, inc="act")
                pl.wait("vector", pl.snap("act"))
                pl.opf("vector", "tensor_tensor", st8[:, 5:6], st8[:, 5:6],
                       st8[:, 4:5], ALU.subtract)
                pl.wait("scalar", pl.snap("dve"))
                pl.op("scalar", "activation", st8[:, 6:7], st8[:, 5:6], AF.Ln)
                pl.op("scalar", "activation", st8[:, 6:7], st8[:, 6:7], AF.Exp,
                      scale=-0.5, inc="act")
                pl.wait("vector", pl.snap("act"))
                pl.opf("vector", "scalar_tensor_tensor", st8[:, 7:8], st8[:, 0:1],
                       1.0 / D, st8[:, 6:7], ALU.mult, ALU.mult)
                pl.opf("vector", "tensor_scalar", se_bf[:], se_x[:],
                       st8[:, 6:7], st8[:, 7:8], ALU.mult, ALU.subtract)
                pl.opf("vector", "tensor_tensor", se_t1[:], se_bf[:], slgb[:], ALU.mult)
                pl.opf("vector", "tensor_tensor", se_t1[:], se_t1[:], slbb[:], ALU.add)
                pl.op("vector", "tensor_scalar", se_t1[:], se_t1[:],
                      sclb[:, 0:1], None, ALU.mult, inc="dve")
                pl.wait("tensor", pl.snap("dve"))
                for d in range(8):
                    pl.op("tensor", "transpose", pre_tp[:, d * 128:(d + 1) * 128],
                          se_t1[:, d * 128:(d + 1) * 128], idb_s[:],
                          inc="pe" if d == 7 else None)
                pl.wait("vector", pl.snap("pe"))
                pl.op("vector", "tensor_copy",
                      _r3(seT, 1024)[:, :, st * 128:(st + 1) * 128],
                      _r3(pre_tp, 128), inc="dve")
                pl.barrier()
            pl.barrier()
        if dump:
            pl.dma("sync", dbg["d_seT"], seT[:], "qs")

        def ffn(src, dst, u):
            with ExitStack() as FS:
                FE = FS.enter_context
                xnT = FE(nc.sbuf_tensor("xnT" + u, [128, ND * 4096], BF16))
                hT = FE(nc.sbuf_tensor("hT" + u, [128, ND * 1024], BF16))
                xrot = FE(nc.sbuf_tensor("xrot" + u, [128, 2 * 1024], F32))
                stt = FE(nc.sbuf_tensor("stt" + u, [128, 16], F32))
                ssc = FE(nc.sbuf_tensor("ssc" + u, [128, 8], F32))
                rcpS = FE(nc.sbuf_tensor("rcpS" + u, [128, 8], F32))
                ring = FE(nc.sbuf_tensor("ring" + u, [128, RING * 2048], BF16))
                AS2 = FE(ExitStack())
                AE2 = AS2.enter_context
                xn = AE2(nc.sbuf_tensor("xn" + u, [128, NT * 1024], BF16))
                dT = AE2(nc.sbuf_tensor("dT" + u, [128, 4096], BF16))
                stg = AE2(nc.sbuf_tensor("stg" + u, [128, 2 * 512], BF16))
                nogb = AE2(nc.sbuf_tensor("nogb" + u, [128, 1024], BF16))
                nobb = AE2(nc.sbuf_tensor("nobb" + u, [128, 1024], BF16))
                xcb = AE2(nc.sbuf_tensor("xcb" + u, [128, 1024], BF16))
                t1b = AE2(nc.sbuf_tensor("t1b" + u, [128, 1024], BF16))
                slot_bf = AE2(nc.sbuf_tensor("slot_bf" + u, [128, 1024], BF16))

                with nc.psum_tensor("bc_ps" + u, [128, 1024], F32) as bc_ps:
                    bcast_vec(nog, nogb, bc_ps, "nog")
                    bcast_vec(nob, nobb, bc_ps, "nob")

                xrot_rel = [None, None]

                with nc.psum_tensor("ln_tp" + u, [128, 2 * 1024], BF16) as ln_tp, \
                     nc.psum_tensor("sqp" + u, [128, 1024], F32) as sq:
                    for i in range(NT):
                        h = i % 2
                        if xrot_rel[h] is not None:
                            pl.wait("scalar", xrot_rel[h])
                        pl.dma("scalar", xrot[:, h * 1024:(h + 1) * 1024],
                               src[i * 128:(i + 1) * 128, :], "qc")
                        ls = pl.snap("qc")
                        xi = xrot[:, h * 1024:(h + 1) * 1024]
                        pl.wait("vector", ls)
                        pl.wait("scalar", ls)
                        pl.opf("vector", "reduce_sum", stt[:, 0:1], xi, axis=AX.X)
                        s1d = pl.snap("dve")
                        pl.op("scalar", "activation", sq[:, 0:1024], xi, AF.Square,
                              accum_out=stt[:, 1:2])
                        pl.wait("scalar", s1d)
                        pl.op("scalar", "activation", stt[:, 4:5], stt[:, 0:1],
                              AF.Square, scale=(1.0 + EPS) ** 0.5 / D)
                        pl.op("scalar", "activation", stt[:, 5:6], stt[:, 1:2],
                              AF.Copy, scale=(1.0 + EPS) / D, bias=EPS * EPS,
                              inc="act")
                        pl.wait("vector", pl.snap("act"))
                        pl.opf("vector", "tensor_tensor", stt[:, 5:6], stt[:, 5:6],
                               stt[:, 4:5], ALU.subtract)
                        pl.wait("scalar", pl.snap("dve"))
                        pl.op("scalar", "activation", stt[:, 6:7], stt[:, 5:6], AF.Ln)
                        pl.op("scalar", "activation", stt[:, 6:7], stt[:, 6:7],
                              AF.Exp, scale=-0.5, inc="act")
                        pl.wait("vector", pl.snap("act"))
                        pl.opf("vector", "scalar_tensor_tensor", stt[:, 7:8],
                               stt[:, 0:1], 1.0 / D, stt[:, 6:7],
                               ALU.mult, ALU.mult)
                        pl.opf("vector", "tensor_scalar", xcb[:], xi,
                               stt[:, 6:7], stt[:, 7:8], ALU.mult, ALU.subtract)
                        xrot_rel[h] = pl.snap("dve")
                        pl.opf("vector", "tensor_tensor", t1b[:], xcb[:], nogb[:],
                               ALU.mult)
                        pl.op("vector", "tensor_tensor",
                              xn[:, i * 1024:(i + 1) * 1024], t1b[:], nobb[:],
                              ALU.add, inc="dve")
                        pl.wait("tensor", pl.snap("dve"))
                        tph = ln_tp[:, (i % 2) * 1024:((i % 2) + 1) * 1024]
                        for d in range(8):
                            pl.op("tensor", "transpose",
                                  tph[:, d * 128:(d + 1) * 128],
                                  xn[:, i * 1024 + d * 128:i * 1024 + (d + 1) * 128],
                                  idb_s[:], inc="pe" if d == 7 else None)
                        pl.wait("vector", pl.snap("pe"))
                        pl.op("vector", "tensor_copy",
                              _r3(xnT, 4096)[:, :, i * 128:(i + 1) * 128],
                              _r3(tph, 128), inc="dve")
                ln_done = pl.snap("dve")
                if dump and u == "A":
                    pl.wait("sync", ln_done)
                    pl.dma("sync", dbg["d_xn"], xn[:], "qs")
                    pl.dma("sync", dbg["d_xnT"], xnT[:], "qs")

                ring_rel = [None] * RING
                qg_slot_end = []
                slot_srcs = []
                for p in range(8):
                    eA, eB = 2 * p, 2 * p + 1
                    for fh in range(2):
                        for k in range(8):
                            slot_srcs.append((
                                wg[eA * 1024 + k * 128: eA * 1024 + (k + 1) * 128,
                                   fh * 1024:(fh + 1) * 1024],
                                wg[eB * 1024 + k * 128: eB * 1024 + (k + 1) * 128,
                                   fh * 1024:(fh + 1) * 1024]))
                    for fh in range(2):
                        for k in range(8):
                            slot_srcs.append((
                                wu[eA * 1024 + k * 128: eA * 1024 + (k + 1) * 128,
                                   fh * 1024:(fh + 1) * 1024],
                                wu[eB * 1024 + k * 128: eB * 1024 + (k + 1) * 128,
                                   fh * 1024:(fh + 1) * 1024]))
                    for fc in range(16):
                        slot_srcs.append((
                            wd[eA * 2048 + fc * 128: eA * 2048 + (fc + 1) * 128, :],
                            wd[eB * 2048 + fc * 128: eB * 2048 + (fc + 1) * 128, :]))

                def emit_slot_dma(j):
                    r = j % RING
                    if ring_rel[r] is not None:
                        pl.wait("gpsimd", ring_rel[r])
                    a_src, b_src = slot_srcs[j]
                    sl = ring[:, r * 2048: r * 2048 + 1024]
                    sr = ring[:, r * 2048 + 1024: (r + 1) * 2048]
                    pl.dma("gpsimd", sl, a_src, "qg")
                    pl.dma("gpsimd", sr, b_src, "qg")
                    qg_slot_end.append(pl.snap("qg"))

                def slot_ap(j):
                    r = j % RING
                    return ring[:, r * 2048:(r + 1) * 2048]

                for j in range(min(RING, len(slot_srcs))):
                    emit_slot_dma(j)
                next_dma = [RING]

                def consume_slot(j, passes_fn):
                    pl.wait("tensor", qg_slot_end[j])
                    passes_fn(slot_ap(j))
                    ring_rel[j % RING] = pl.snap("pe")
                    if next_dma[0] < len(slot_srcs):
                        emit_slot_dma(next_dma[0])
                        next_dma[0] += 1

                for m in range(8):
                    with nc.psum_tensor("lps%d" % m + u, [128, 4096], F32) as lps:
                        pl.wait("tensor", ln_done)
                        for d in range(8):
                            for nb in range(8):
                                pl.op("tensor", "matmul",
                                      lps[:, nb * 512:(nb + 1) * 512],
                                      seT[:, d * 1024 + m * 128:
                                          d * 1024 + (m + 1) * 128],
                                      xnT[:, d * 4096 + nb * 512:
                                          d * 4096 + (nb + 1) * 512],
                                      start=(d == 0), stop=(d == 7),
                                      inc="pe" if (d == 7 and nb == 7) else None)
                        pl.wait("vector", pl.snap("pe"))
                        pl.op("vector", "reduce_max", stt[:, 8:9], lps[:],
                              axis=AX.X, negate=True, inc="dve")
                        negmx = pl.snap("dve")
                        lps_bf = lps[:].bitcast(BF16)
                        stg_rel = [None, None]
                        for j in range(8):
                            hh = j % 2
                            pl.wait("scalar", negmx)
                            if stg_rel[hh] is not None:
                                pl.wait("scalar", stg_rel[hh])
                            pl.op("scalar", "activation",
                                  stg[:, hh * 512:(hh + 1) * 512],
                                  lps[:, j * 512:(j + 1) * 512], AF.Exp,
                                  bias=stt[:, 8:9], accum_out=ssc[:, j:j + 1],
                                  inc="act")
                            pl.wait("tensor", pl.snap("act"))
                            for q in range(4):
                                pl.op("tensor", "transpose",
                                      lps_bf[:, j * 1024 + q * 128:
                                             j * 1024 + (q + 1) * 128],
                                      stg[:, hh * 512 + q * 128:
                                          hh * 512 + (q + 1) * 128],
                                      idb_s[:], inc="pe" if q == 3 else None)
                            stg_rel[hh] = pl.snap("pe")
                            pl.wait("vector", pl.snap("pe"))
                            pl.op("vector", "tensor_copy",
                                  dT[:, j * 512:(j + 1) * 512],
                                  lps_bf[:, j * 1024: j * 1024 + 512], inc="dve")
                        pl.opf("vector", "reduce_sum", stt[:, 9:10], ssc[:, 0:8],
                               axis=AX.X)
                        pl.opf("vector", "reciprocal", rcpS[:, m:m + 1],
                               stt[:, 9:10])
                        pl.op("vector", "engine_nop", inc="dve")
                        dT_done = pl.snap("dve")
                        if dump and u == "A" and m == 0:
                            pl.wait("sync", dT_done)
                            pl.dma("sync", dbg["d_dT"], dT[:], "qs")
                            pl.dma("sync", dbg["d_stt"], stt[:], "qs")
                            pl.dma("sync", dbg["d_ssc"], ssc[:], "qs")
                    with nc.psum_tensor("sps%d" % m + u, [128, 1024], F32) as sps, \
                         nc.psum_tensor("stp%d" % m + u, [128, 1024], BF16) as stp:
                        pl.wait("tensor", dT_done)
                        for k in range(NT):
                            for nh in range(2):
                                pl.op("tensor", "matmul",
                                      sps[:, nh * 512:(nh + 1) * 512],
                                      dT[:, k * 128:(k + 1) * 128],
                                      xn[:, k * 1024 + nh * 512:
                                         k * 1024 + (nh + 1) * 512],
                                      start=(k == 0), stop=(k == NT - 1),
                                      inc="pe" if (k == NT - 1 and nh == 1) else None)
                        pl.wait("vector", pl.snap("pe"))
                        pl.op("vector", "tensor_scalar", slot_bf[:], sps[:],
                              rcpS[:, m:m + 1], None, ALU.mult, inc="dve")
                        pl.wait("tensor", pl.snap("dve"))
                        for d in range(8):
                            pl.op("tensor", "transpose",
                                  stp[:, d * 128:(d + 1) * 128],
                                  slot_bf[:, d * 128:(d + 1) * 128], idb_s[:],
                                  inc="pe" if d == 7 else None)
                        pl.wait("vector", pl.snap("pe"))
                        pl.op("vector", "tensor_copy",
                              _r3(hT, 1024)[:, :, m * 128:(m + 1) * 128],
                              _r3(stp, 128), inc="dve")
                        ln_done = pl.snap("dve")
                        if dump and u == "A" and m == 0:
                            pl.wait("sync", pl.snap("dve"))
                            pl.dma("sync", dbg["d_slot"], slot_bf[:], "qs")
                        if dump and u == "A" and m == 7:
                            pl.wait("sync", pl.snap("dve"))
                            pl.dma("sync", dbg["d_hT"], hT[:], "qs")

                pl.barrier()
                AS2.close()

                eout = FE(nc.sbuf_tensor("eout" + u, [128, ND * 1024], BF16))
                with nc.sbuf_tensor("gsb" + u, [128, 2048], BF16) as gsb, \
                     nc.sbuf_tensor("usb" + u, [128, 2048], BF16) as usb, \
                     nc.sbuf_tensor("gub" + u, [128, 2048], BF16) as gub, \
                     nc.sbuf_tensor("guT" + u, [128, 2048], BF16) as guT, \
                     nc.sbuf_tensor("bstgA" + u, [1, 5120], BF16) as bstgA, \
                     nc.sbuf_tensor("bstgB" + u, [1, 5120], BF16) as bstgB, \
                     nc.psum_tensor("gpA" + u, [128, 1024], F32) as gpA, \
                     nc.psum_tensor("gpB" + u, [128, 1024], F32) as gpB, \
                     nc.psum_tensor("gtp" + u, [128, 2048], BF16) as gtp:
                    jbase = 0
                    eout_rel = None
                    usb_rel = None
                    bias_rel = None
                    npairs = 0 if "noexp" in skip else (1 if "exp0" in skip else 8)
                    for p in range(npairs):
                        if "biasdma" in skip:
                            bias_ld = pl.snap("qg")
                        if bias_rel is not None and "biasdma" not in skip:
                            pl.wait("gpsimd", bias_rel)
                        if "biasdma" not in skip:
                            pl.dma("gpsimd", bstgA[0:1, 0:2048],
                                   bgE[0:1, p * 2048:(p + 1) * 2048], "qg")
                        if "biasdma" not in skip:
                            pl.dma("gpsimd", bstgB[0:1, 0:2048],
                                   bgE[1:2, p * 2048:(p + 1) * 2048], "qg")
                            pl.dma("gpsimd", bstgA[0:1, 2048:4096],
                                   buE[0:1, p * 2048:(p + 1) * 2048], "qg")
                            pl.dma("gpsimd", bstgB[0:1, 2048:4096],
                                   buE[1:2, p * 2048:(p + 1) * 2048], "qg")
                            pl.dma("gpsimd", bstgA[0:1, 4096:5120],
                                   bdE[0:1, p * 1024:(p + 1) * 1024], "qg")
                            pl.dma("gpsimd", bstgB[0:1, 4096:5120],
                                   bdE[1:2, p * 1024:(p + 1) * 1024], "qg")
                            bias_ld = pl.snap("qg")
                        mats = (("g", 0),) if ("gateonly" in skip) else \
                            (("g", 0), ("u", 2048))
                        for mat, boff in mats:
                            if mat == "g" and eout_rel is not None:
                                pl.wait("tensor", eout_rel)
                            if mat == "u":
                                pl.wait("tensor", gdone)
                            for fh in ((0,) if "fh0" in skip else (0, 1)):
                                if fh == 1:
                                    pl.wait("tensor",
                                            gdone if mat == "g" else usb_rel)
                                for k in range(8):
                                    j = jbase + fh * 8 + k

                                    def passes(sl, p=p, k=k):
                                        lh = hT[:, k * 1024 + p * 128:
                                                k * 1024 + (p + 1) * 128]
                                        for ab, pc in ((0, gpA), (1, gpB)):
                                            for nh in range(2):
                                                pl.op(
                                                    "tensor", "matmul",
                                                    pc[:, nh * 512:(nh + 1) * 512],
                                                    lh,
                                                    sl[:, ab * 1024 + nh * 512:
                                                       ab * 1024 + (nh + 1) * 512],
                                                    start=(k == 0), stop=False,
                                                    inc="pe" if (ab == 1 and
                                                                 nh == 1) else None)
                                    consume_slot(j, passes)
                                if "bias" not in skip:
                                    pl.wait("tensor", bias_ld)
                                    for ab, pc, bs in ((0, gpA, bstgA), (1, gpB, bstgB)):
                                        for nh in range(2):
                                            pl.op("tensor", "matmul",
                                                  pc[:, nh * 512:(nh + 1) * 512],
                                                  one1_s[0:1, :],
                                                  bs[0:1, boff + fh * 1024 + nh * 512:
                                                     boff + fh * 1024 + (nh + 1) * 512],
                                                  start=False, stop=True,
                                                  inc="pe" if (ab == 1 and
                                                               nh == 1) else None)
                                if mat == "g":
                                    if "nogelu" in skip:
                                        pl.op("scalar", "nop", inc="act")
                                        gdone = pl.snap("act")
                                    else:
                                        pl.wait("scalar", pl.snap("pe"))
                                        pl.op("scalar", "activation",
                                              gsb[0:64, fh * 1024:(fh + 1) * 1024],
                                              gpA[0:64, :], AF.Gelu)
                                        pl.op("scalar", "activation",
                                              gsb[64:128, fh * 1024:(fh + 1) * 1024],
                                              gpB[64:128, :], AF.Gelu, inc="act")
                                        gdone = pl.snap("act")
                                    if "gelubar" in skip:
                                        pl.barrier()
                                else:
                                    pl.wait("vector", pl.snap("pe"))
                                    pl.op("vector", "tensor_copy",
                                          usb[0:64, fh * 1024:(fh + 1) * 1024],
                                          gpA[0:64, :])
                                    pl.opf("vector", "tensor_copy",
                                           usb[64:128, fh * 1024:(fh + 1) * 1024],
                                           gpB[64:128, :])
                                    usb_rel = pl.snap("dve")
                            jbase += 16
                        if "gateonly" in skip:
                            jbase += 32
                            continue
                        pl.wait("vector", gdone)
                        pl.op("vector", "tensor_tensor", gub[:], gsb[:],
                              usb[:], ALU.mult, inc="dve")
                        pl.wait("tensor", pl.snap("dve"))
                        pl.wait("tensor", usb_rel)
                        for fc in range(16):
                            pl.op("tensor", "transpose",
                                  gtp[:, fc * 128:(fc + 1) * 128],
                                  gub[:, fc * 128:(fc + 1) * 128], idb_s[:],
                                  inc="pe" if fc == 15 else None)
                        pl.wait("vector", pl.snap("pe"))
                        pl.op("vector", "tensor_copy", guT[:], gtp[:], inc="dve")
                        pl.wait("tensor", pl.snap("dve"))
                        if eout_rel is not None:
                            pl.wait("tensor", eout_rel)
                        if "nodown" in skip:
                            jbase += 16
                            continue
                        for fc in range(16):
                            j = jbase + fc

                            def dpasses(sl, fc=fc):
                                lh = guT[:, fc * 128:(fc + 1) * 128]
                                for ab, pc in ((0, gpA), (1, gpB)):
                                    for nh in range(2):
                                        pl.op("tensor", "matmul",
                                              pc[:, nh * 512:(nh + 1) * 512],
                                              lh,
                                              sl[:, ab * 1024 + nh * 512:
                                                 ab * 1024 + (nh + 1) * 512],
                                              start=(fc == 0), stop=False,
                                              inc="pe" if (ab == 1 and
                                                           nh == 1) else None)
                            consume_slot(j, dpasses)
                        jbase += 16
                        if "bias" not in skip:
                            for ab, pc, bs in ((0, gpA, bstgA), (1, gpB, bstgB)):
                                for nh in range(2):
                                    pl.op("tensor", "matmul",
                                          pc[:, nh * 512:(nh + 1) * 512],
                                          one1_s[0:1, :],
                                          bs[0:1, 4096 + nh * 512:
                                             4096 + (nh + 1) * 512],
                                          start=False, stop=True,
                                          inc="pe" if (ab == 1 and nh == 1) else None)
                        bias_rel = pl.snap("pe")
                        pl.wait("vector", pl.snap("pe"))
                        pl.op("vector", "tensor_copy",
                              eout[0:64, p * 1024:(p + 1) * 1024], gpA[0:64, :])
                        pl.op("vector", "tensor_copy",
                              eout[64:128, p * 1024:(p + 1) * 1024],
                              gpB[64:128, :], inc="dve")
                        eout_rel = pl.snap("dve")
                if dump and u == "A" and eout_rel is not None:
                    pl.wait("sync", eout_rel)
                    pl.dma("sync", dbg["d_eout"], eout[:], "qs")
                with nc.sbuf_tensor("cstg" + u, [128, 1024], BF16) as cstg, \
                     nc.sbuf_tensor("cTs" + u, [128, 1024], BF16) as cTs, \
                     nc.sbuf_tensor("yrot" + u, [128, 2 * 1024], F32) as yrot, \
                     nc.psum_tensor("cps" + u, [128, 2 * 1024], F32) as cps, \
                     nc.psum_tensor("ctp" + u, [128, 1024], BF16) as ctp, \
                     nc.psum_tensor("ops" + u, [128, 1024], F32) as ops:
                    yst_rel = [None, None]
                    cstg_rel = None
                    cT_rel = None
                    ops_rel = None
                    exp_rel = [None, None]
                    if eout_rel is not None:
                        pl.wait("tensor", eout_rel)

                    def emit_logits2(i):
                        h = i % 2
                        cph = cps[:, h * 1024:(h + 1) * 1024]
                        if exp_rel[h] is not None:
                            pl.wait("tensor", exp_rel[h])
                        for d in range(8):
                            for nh in range(2):
                                pl.op("tensor", "matmul",
                                      cph[:, nh * 512:(nh + 1) * 512],
                                      xnT[:, d * 4096 + i * 128:
                                          d * 4096 + (i + 1) * 128],
                                      seT[:, d * 1024 + nh * 512:
                                          d * 1024 + (nh + 1) * 512],
                                      start=(d == 0), stop=(d == 7),
                                      inc="pe" if (d == 7 and nh == 1) else None)
                        return pl.snap("pe")

                    log_done = [None] * NT
                    log_done[0] = emit_logits2(0)
                    for i in range(NT):
                        h = i % 2
                        if xrot_rel[h] is not None:
                            pl.wait("scalar", xrot_rel[h])
                        pl.dma("scalar", xrot[:, h * 1024:(h + 1) * 1024],
                               src[i * 128:(i + 1) * 128, :], "qc")
                        xls = pl.snap("qc")
                        cph = cps[:, h * 1024:(h + 1) * 1024]
                        if i + 1 < NT:
                            log_done[i + 1] = emit_logits2(i + 1)
                        pl.wait("vector", log_done[i])
                        pl.op("vector", "reduce_max", stt[:, 10:11], cph,
                              axis=AX.X, negate=True, inc="dve")
                        pl.wait("scalar", pl.snap("dve"))
                        if cstg_rel is not None:
                            pl.wait("scalar", cstg_rel)
                        pl.op("scalar", "activation", cstg[:], cph, AF.Exp,
                              bias=stt[:, 10:11], accum_out=stt[:, 11:12],
                              inc="act")
                        exp_rel[h] = pl.snap("act")
                        pl.wait("vector", pl.snap("act"))
                        pl.opf("vector", "reciprocal", stt[:, 12:13], stt[:, 11:12])
                        pl.opf("vector", "tensor_scalar", stt[:, 13:14],
                               stt[:, 12:13], 0.5, None, ALU.mult)
                        pl.wait("tensor", pl.snap("act"))
                        if cT_rel is not None:
                            pl.wait("tensor", cT_rel)
                        for q in range(8):
                            pl.op("tensor", "transpose",
                                  ctp[:, q * 128:(q + 1) * 128],
                                  cstg[:, q * 128:(q + 1) * 128], idb_s[:],
                                  inc="pe" if q == 7 else None)
                        cstg_rel = pl.snap("pe")
                        pl.wait("vector", pl.snap("pe"))
                        pl.op("vector", "tensor_copy", cTs[:], ctp[:], inc="dve")
                        pl.wait("tensor", pl.snap("dve"))
                        if ops_rel is not None:
                            pl.wait("tensor", ops_rel)
                        for k in range(8):
                            for nh in range(2):
                                pl.op("tensor", "matmul",
                                      ops[:, nh * 512:(nh + 1) * 512],
                                      cTs[:, k * 128:(k + 1) * 128],
                                      eout[:, k * 1024 + nh * 512:
                                           k * 1024 + (nh + 1) * 512],
                                      start=(k == 0), stop=(k == 7),
                                      inc="pe" if (k == 7 and nh == 1) else None)
                        cT_rel = pl.snap("pe")
                        pl.wait("vector", pl.snap("pe"))
                        pl.wait("vector", xls)
                        if yst_rel[h] is not None:
                            pl.wait("vector", yst_rel[h])
                        pl.op("vector", "scalar_tensor_tensor",
                              yrot[:, h * 1024:(h + 1) * 1024], ops[:],
                              stt[:, 13:14], xrot[:, h * 1024:(h + 1) * 1024],
                              ALU.mult, ALU.add, inc="dve")
                        xrot_rel[h] = pl.snap("dve")
                        ops_rel = pl.snap("dve")
                        pl.wait("scalar", pl.snap("dve"))
                        pl.dma("scalar", dst[i * 128:(i + 1) * 128, :],
                               yrot[:, h * 1024:(h + 1) * 1024], "qc")
                        yst_rel[h] = pl.snap("qc")
            pl.barrier()

        def attn(src, dst):
            with ExitStack() as AS:
                AE = AS.enter_context
                wbf = AE(nc.sbuf_tensor("wbf", [128, 4 * 8192], BF16))
                lngb = AE(nc.sbuf_tensor("lngb", [128, 1024], BF16))
                lnbb = AE(nc.sbuf_tensor("lnbb", [128, 1024], BF16))
                xln = AE(nc.sbuf_tensor("xln", [128, 2 * 1024], F32))
                xrs = AE(nc.sbuf_tensor("xrs", [128, 2 * 1024], F32))
                xa_stg = AE(nc.sbuf_tensor("xa_stg", [128, 1024], BF16))
                xaT = AE(nc.sbuf_tensor("xaT", [128, 2 * 4096], BF16))
                lastT = AE(nc.sbuf_tensor("lastT", [128, 512], BF16))
                dbuf = AE(nc.sbuf_tensor("dbuf", [128, 512], BF16))
                kinT = AE(nc.sbuf_tensor("kinT", [128, 4096], BF16))
                vinT = AE(nc.sbuf_tensor("vinT", [128, 4096], BF16))
                rinT = AE(nc.sbuf_tensor("rinT", [128, 4096], BF16))
                ekT = AE(nc.sbuf_tensor("ekT", [128, 4096], BF16))
                vT = AE(nc.sbuf_tensor("vT", [128, 4096], BF16))
                sgT = AE(nc.sbuf_tensor("sgT", [128, 4096], BF16))
                ubuf = AE(nc.sbuf_tensor("ubuf", [128, 512], F32))
                wbuf2 = AE(nc.sbuf_tensor("wbuf2", [128, 512], F32))
                rcb = AE(nc.sbuf_tensor("rcb", [128, 512], F32))
                numT = AE(nc.sbuf_tensor("numT", [128, 4096], BF16))
                denT = AE(nc.sbuf_tensor("denT", [128, 4096], BF16))
                gatedT = AE(nc.sbuf_tensor("gatedT", [128, 4096], BF16))
                carry = AE(nc.sbuf_tensor("carry", [128, 8], BF16))
                snum = AE(nc.sbuf_tensor("snum", [128, 8], F32))
                sden = AE(nc.sbuf_tensor("sden", [128, 8], F32))
                stt2 = AE(nc.sbuf_tensor("stt2", [128, 16], F32))
                sqa = AE(nc.sbuf_tensor("sqa2", [128, 1024], BF16))
                with nc.psum_tensor("bc2x", [128, 1024], F32) as bc2:
                    bcast_vec(lng, lngb, bc2, "lng")
                    bcast_vec(lnb, lnbb, bc2, "lnb")
                atp = AE(nc.psum_tensor("atp", [128, 2048], BF16))
                prjH = AE(nc.psum_tensor("prjH", [128, 2048], F32))
                ops2 = AE(nc.psum_tensor("ops2", [128, 1024], F32))
                for w_i, wsrc in enumerate((wkT, wvT, wrT, woT)):
                    for ck in range(8):
                        pl.dma("gpsimd",
                               wbf[:, w_i * 8192 + ck * 1024:
                                   w_i * 8192 + (ck + 1) * 1024],
                               wsrc[ck * 128:(ck + 1) * 128, :], "qg")
                wld = pl.snap("qg")
                pl.op("vector", "memset", carry[:], 0.0)
                pl.op("vector", "memset", snum[:], 0.0)
                pl.op("vector", "memset", sden[:], 0.0, inc="dve")
                init_done = pl.snap("dve")
                pl.wait("tensor", wld)
                pl.wait("tensor", init_done)

                xln_rel = [None, None]
                xrs_rel = [None, None]
                prj_carry = [None]
                ops_carry = [None]
                mix_done = [None] * (NCH + 1)
                proj_rel = [None] * NCH
                atp_rel = [None, None]
                yst_rel = [None, None]

                def stage_ln(ch):
                    par = ch % 2
                    for tsub in range(4):
                        i = ch * 4 + tsub
                        h = i % 2
                        if xln_rel[h] is not None:
                            pl.wait("scalar", xln_rel[h])
                        pl.dma("scalar", xln[:, h * 1024:(h + 1) * 1024],
                               src[i * 128:(i + 1) * 128, :], "qc")
                        ls = pl.snap("qc")
                        xi = xln[:, h * 1024:(h + 1) * 1024]
                        pl.wait("vector", ls)
                        pl.wait("scalar", ls)
                        pl.opf("vector", "reduce_sum", stt2[:, 0:1], xi,
                               axis=AX.X)
                        s1d = pl.snap("dve")
                        pl.op("scalar", "activation", sqa[:, 0:1024], xi,
                              AF.Square, accum_out=stt2[:, 1:2])
                        pl.wait("scalar", s1d)
                        pl.op("scalar", "activation", stt2[:, 4:5],
                              stt2[:, 0:1], AF.Square,
                              scale=(1.0 + EPS) ** 0.5 / D)
                        pl.op("scalar", "activation", stt2[:, 5:6],
                              stt2[:, 1:2], AF.Copy,
                              scale=(1.0 + EPS) / D, bias=EPS * EPS,
                              inc="act")
                        pl.wait("vector", pl.snap("act"))
                        pl.opf("vector", "tensor_tensor", stt2[:, 5:6],
                               stt2[:, 5:6], stt2[:, 4:5], ALU.subtract)
                        pl.wait("scalar", pl.snap("dve"))
                        pl.op("scalar", "activation", stt2[:, 6:7],
                              stt2[:, 5:6], AF.Ln)
                        pl.op("scalar", "activation", stt2[:, 6:7],
                              stt2[:, 6:7], AF.Exp, scale=-0.5, inc="act")
                        pl.wait("vector", pl.snap("act"))
                        pl.opf("vector", "scalar_tensor_tensor", stt2[:, 7:8],
                               stt2[:, 0:1], 1.0 / D, stt2[:, 6:7],
                               ALU.mult, ALU.mult)
                        pl.opf("vector", "tensor_scalar", xa_stg[:], xi,
                               stt2[:, 6:7], stt2[:, 7:8],
                               ALU.mult, ALU.subtract)
                        xln_rel[h] = pl.snap("dve")
                        pl.opf("vector", "tensor_tensor", xa_stg[:],
                               xa_stg[:], lngb[:], ALU.mult)
                        pl.op("vector", "tensor_tensor", xa_stg[:],
                              xa_stg[:], lnbb[:], ALU.add, inc="dve")
                        pl.wait("tensor", pl.snap("dve"))
                        tph = atp[:, (tsub % 2) * 1024:((tsub % 2) + 1) * 1024]
                        if atp_rel[tsub % 2] is not None:
                            pl.wait("tensor", atp_rel[tsub % 2])
                        for c in range(8):
                            pl.op("tensor", "transpose",
                                  tph[:, c * 128:(c + 1) * 128],
                                  xa_stg[:, c * 128:(c + 1) * 128],
                                  idb_s[:], inc="pe" if c == 7 else None)
                        pl.wait("vector", pl.snap("pe"))
                        pl.op("vector", "tensor_copy",
                              _r3(xaT, 512)[:, par * 8:(par + 1) * 8,
                                            tsub * 128:(tsub + 1) * 128],
                              _r3(tph, 128), inc="dve")
                        atp_rel[tsub % 2] = pl.snap("dve")
                    if ch >= 1:
                        proj_rel[ch - 1] = pl.wait_lazy("vector")
                    for c in range(8):
                        xc_ap = xaT[:, par * 4096 + c * 512:
                                    par * 4096 + (c + 1) * 512]
                        pl.op("vector", "tensor_copy", lastT[:, 0:1],
                              carry[:, c:c + 1])
                        pl.opf("vector", "tensor_copy", lastT[:, 1:512],
                               xaT[:, par * 4096 + c * 512:
                                   par * 4096 + c * 512 + 511])
                        pl.op("vector", "tensor_copy", carry[:, c:c + 1],
                              xaT[:, par * 4096 + c * 512 + 511:
                                  par * 4096 + c * 512 + 512])
                        pl.opf("vector", "tensor_tensor", dbuf[:], xc_ap,
                               lastT[:], ALU.subtract)
                        pl.op("vector", "scalar_tensor_tensor",
                              kinT[:, c * 512:(c + 1) * 512], dbuf[:],
                              tmk_s[:, c:c + 1], lastT[:], ALU.mult, ALU.add)
                        pl.op("vector", "scalar_tensor_tensor",
                              vinT[:, c * 512:(c + 1) * 512], dbuf[:],
                              tmv_s[:, c:c + 1], lastT[:], ALU.mult, ALU.add)
                        pl.op("vector", "scalar_tensor_tensor",
                              rinT[:, c * 512:(c + 1) * 512], dbuf[:],
                              tmr_s[:, c:c + 1], lastT[:], ALU.mult, ALU.add,
                              inc="dve")
                    mix_done[ch] = pl.snap("dve")

                def stage_compute(ch):
                    pl.wait("tensor", mix_done[ch])
                    ev_prev = prj_carry[0]

                    def proj(w_i, src_t, ev_fn):
                        nonlocal ev_prev
                        for half in range(2):
                            if ev_prev is not None:
                                pl.wait("tensor", ev_prev)
                            for mo4 in range(4):
                                mo = half * 4 + mo4
                                for kc in range(8):
                                    pl.op("tensor", "matmul",
                                          prjH[:, mo4 * 512:(mo4 + 1) * 512],
                                          wbf[:, w_i * 8192 + kc * 1024 + mo * 128:
                                              w_i * 8192 + kc * 1024 + (mo + 1) * 128],
                                          src_t[:, kc * 512:(kc + 1) * 512],
                                          start=(kc == 0), stop=(kc == 7),
                                          inc="pe" if kc == 7 else None)
                                ev_fn(mo, mo4, pl.snap("pe"))
                            ev_prev = ev_fn(None, None, None)

                    def ek_ev(mo, mo4, s):
                        if mo is None:
                            return pl.snap("act")
                        pl.wait("scalar", s)
                        pl.op("scalar", "activation",
                              ekT[:, mo * 512:(mo + 1) * 512],
                              prjH[:, mo4 * 512:(mo4 + 1) * 512], AF.Exp,
                              inc="act")
                    proj(0, kinT, ek_ev)

                    def v_ev(mo, mo4, s):
                        if mo is None:
                            return pl.snap("act")
                        pl.wait("scalar", s)
                        pl.op("scalar", "activation",
                              vT[:, mo * 512:(mo + 1) * 512],
                              prjH[:, mo4 * 512:(mo4 + 1) * 512], AF.Copy,
                              inc="act")
                    proj(1, vinT, v_ev)

                    def r_ev(mo, mo4, s):
                        if mo is None:
                            return pl.snap("act")
                        pl.wait("scalar", s)
                        pl.op("scalar", "activation",
                              sgT[:, mo * 512:(mo + 1) * 512],
                              prjH[:, mo4 * 512:(mo4 + 1) * 512], AF.Sigmoid,
                              inc="act")
                    proj(2, rinT, r_ev)
                    r_done = pl.snap("act")
                    prj_carry[0] = r_done
                    if proj_rel[ch] is not None:
                        pl.fill_lazy(proj_rel[ch], "pe")

                    for c in range(8):
                        if c == 0:
                            pl.wait("vector", r_done)
                        td_b = tdE[:, c:c + 1].broadcast_to([128, 512])
                        pl.opf("vector", "tensor_tensor", ubuf[:],
                               ekT[:, c * 512:(c + 1) * 512],
                               vT[:, c * 512:(c + 1) * 512], ALU.mult)
                        pl.op("vector", "tensor_tensor_scan",
                              numT[:, c * 512:(c + 1) * 512], td_b,
                              ubuf[:], snum[:, c:c + 1], ALU.mult, ALU.add)
                        pl.opf("vector", "tensor_tensor_scan",
                               denT[:, c * 512:(c + 1) * 512], td_b,
                               ekT[:, c * 512:(c + 1) * 512],
                               sden[:, c:c + 1], ALU.mult, ALU.add)
                        pl.op("vector", "tensor_copy", snum[:, c:c + 1],
                              numT[:, c * 512 + 511: c * 512 + 512])
                        pl.opf("vector", "tensor_copy", sden[:, c:c + 1],
                               denT[:, c * 512 + 511: c * 512 + 512])
                        pl.opf("vector", "tensor_scalar", rcb[:],
                               denT[:, c * 512:(c + 1) * 512],
                               1e-8, None, ALU.add)
                        pl.opf("vector", "reciprocal", rcb[:], rcb[:])
                        pl.opf("vector", "tensor_tensor", wbuf2[:],
                               numT[:, c * 512:(c + 1) * 512],
                               rcb[:], ALU.mult)
                        pl.op("vector", "tensor_tensor",
                              gatedT[:, c * 512:(c + 1) * 512], wbuf2[:],
                              sgT[:, c * 512:(c + 1) * 512], ALU.mult,
                              inc="dve")
                    gated_done = pl.snap("dve")

                    pl.wait("tensor", gated_done)
                    ops_rel2 = ops_carry[0]
                    for tsub in range(4):
                        i = ch * 4 + tsub
                        h = i % 2
                        if xrs_rel[h] is not None:
                            pl.wait("scalar", xrs_rel[h])
                        pl.dma("scalar", xrs[:, h * 1024:(h + 1) * 1024],
                               src[i * 128:(i + 1) * 128, :], "qc")
                        xrl = pl.snap("qc")
                        if ops_rel2 is not None:
                            pl.wait("tensor", ops_rel2)
                        for c in range(8):
                            for nh in range(2):
                                pl.op("tensor", "matmul",
                                      ops2[:, nh * 512:(nh + 1) * 512],
                                      gatedT[:, c * 512 + tsub * 128:
                                             c * 512 + (tsub + 1) * 128],
                                      wbf[:, 3 * 8192 + c * 1024 + nh * 512:
                                          3 * 8192 + c * 1024 + (nh + 1) * 512],
                                      start=(c == 0), stop=(c == 7),
                                      inc="pe" if (c == 7 and nh == 1) else None)
                        pl.wait("vector", pl.snap("pe"))
                        pl.wait("vector", xrl)
                        if yst_rel[h] is not None:
                            pl.wait("vector", yst_rel[h])
                        xt = xrs[:, h * 1024:(h + 1) * 1024]
                        pl.op("vector", "tensor_tensor", xt, ops2[:], xt,
                              ALU.add, inc="dve")
                        ops_rel2 = pl.snap("dve")
                        ops_carry[0] = ops_rel2
                        xrs_rel[h] = pl.snap("dve")
                        pl.wait("scalar", pl.snap("dve"))
                        pl.dma("scalar",
                               dst[i * 128:(i + 1) * 128, :], xt, "qc")
                        yst_rel[h] = pl.snap("qc")

                stage_ln(0)
                for ch in range(NCH):
                    if ch + 1 < NCH:
                        stage_ln(ch + 1)
                    stage_compute(ch)
            pl.barrier()


        ffn(x_in, y if dbg_stage == 1 else xres, "A")
        if dbg_stage >= 2:
            attn(xres, y if dbg_stage == 2 else xres)
        if dbg_stage >= 3:
            ffn(xres, y, "B")

        pl.wait("scalar", pl.snap("qc"))
        pl.wait("gpsimd", pl.snap("qg"))

        with nc.Block() as block:
            @block.sync
            def _(eng):
                for f in pl.streams["sync"]:
                    f(eng)

            @block.tensor
            def _(eng):
                for f in pl.streams["tensor"]:
                    f(eng)

            @block.vector
            def _(eng):
                for f in pl.streams["vector"]:
                    f(eng)

            @block.scalar
            def _(eng):
                for f in pl.streams["scalar"]:
                    f(eng)

            @block.gpsimd
            def _(eng):
                for f in pl.streams["gpsimd"]:
                    f(eng)

    return nc


_NC_CACHE = {}


def _get_nc(dbg_stage=3, dump=False, paranoid=False, skip=()):
    key = (dbg_stage, dump, paranoid, tuple(skip))
    if key not in _NC_CACHE:
        _NC_CACHE[key] = build(dbg_stage, dump, paranoid, skip)
    return _NC_CACHE[key]


def kernel(L0_attn_x, L0_attn_time_decay, L0_attn_time_mix_k, L0_attn_time_mix_v,
           L0_attn_time_mix_r, L0_attn_Wk, L0_attn_Wv, L0_attn_Wr, L0_attn_Wo,
           L0_attn_ln_g, L0_attn_ln_b, L0_ffn_norm_g, L0_ffn_norm_b, L0_ffn_slot_g,
           L0_ffn_slot_b, L0_ffn_scale, L0_ffn_slot_embeds, L0_ffn_Wg, L0_ffn_bg,
           L0_ffn_Wu, L0_ffn_bu, L0_ffn_Wd, L0_ffn_bd, dbg_stage=3, dump=False,
           paranoid=False, skip=()):
    f32 = np.float32
    C = np.ascontiguousarray

    def vecP(v):
        return C(np.asarray(v, f32).reshape(8, 128).T)

    def rowE(b, w):
        a = np.asarray(b, f32).reshape(8, 2, w).transpose(1, 0, 2).reshape(2, 8 * w)
        return C(a)

    common = {
        "wkT": C(np.asarray(L0_attn_Wk, f32).T),
        "wvT": C(np.asarray(L0_attn_Wv, f32).T),
        "wrT": C(np.asarray(L0_attn_Wr, f32).T),
        "woT": C(np.asarray(L0_attn_Wo, f32).T),
        "wg": C(np.asarray(L0_ffn_Wg, f32).reshape(NE * D, FF)),
        "wu": C(np.asarray(L0_ffn_Wu, f32).reshape(NE * D, FF)),
        "wd": C(np.asarray(L0_ffn_Wd, f32).reshape(NE * FF, D)),
        "semb": C(np.asarray(L0_ffn_slot_embeds, f32).reshape(1024, D)),
        "bgE": rowE(L0_ffn_bg, FF),
        "buE": rowE(L0_ffn_bu, FF),
        "bdE": rowE(L0_ffn_bd, D),
        "tdP": vecP(L0_attn_time_decay),
        "tmkP": vecP(L0_attn_time_mix_k),
        "tmvP": vecP(L0_attn_time_mix_v),
        "tmrP": vecP(L0_attn_time_mix_r),
        "lng": C(np.asarray(L0_attn_ln_g, f32).reshape(1, D)),
        "lnb": C(np.asarray(L0_attn_ln_b, f32).reshape(1, D)),
        "nog": C(np.asarray(L0_ffn_norm_g, f32).reshape(1, D)),
        "nob": C(np.asarray(L0_ffn_norm_b, f32).reshape(1, D)),
        "slg": C(np.asarray(L0_ffn_slot_g, f32).reshape(1, D)),
        "slb": C(np.asarray(L0_ffn_slot_b, f32).reshape(1, D)),
        "scl": C(np.asarray(L0_ffn_scale, f32).reshape(1, 1)),
        "idb": np.eye(128, dtype=bf),
        "e2": np.concatenate([
            np.concatenate([np.ones((1, 64)), np.zeros((1, 64))], 1),
            np.concatenate([np.zeros((1, 64)), np.ones((1, 64))], 1)],
            0).astype(bf),
        "one1": np.ones((1, 128), dtype=bf),
    }
    xs = np.asarray(L0_attn_x, f32)
    in_maps = [dict(common, x=C(xs[c])) for c in range(8)]
    nc = _get_nc(dbg_stage, dump, paranoid, skip)
    res = run_bass_kernel_spmd(nc, in_maps, core_ids=list(range(8)))
    out = np.stack([res.results[c]["y"] for c in range(8)], axis=0)
    if dump:
        return out.astype(np.float32), res.results
    return out.astype(np.float32)
